# version 41 (final)
# baseline (speedup 1.0000x reference)
import contextlib
import numpy as np

import concourse.bass as bass
import concourse.mybir as mybir
from concourse.bass_utils import run_bass_kernel_spmd

F32 = mybir.dt.float32
BF16 = mybir.dt.bfloat16
AX = mybir.AxisListType.X
AF = mybir.ActivationFunctionType
ALU = mybir.AluOpType

D = 1024
H = 16
DH = 64
NCC = 8
NP = 8


def OPT(tag):
    return ("OPT", tag)


class Prog:
    ENGINES = ("tensor", "vector", "scalar", "gpsimd", "sync")

    def __init__(self, nc, es):
        self.nc = nc
        self.es = es
        self.items = []
        self.sems = {}

    def _sem(self, name):
        if name not in self.sems:
            self.sems[name] = self.es.enter_context(self.nc.semaphore(name))
        return self.sems[name]

    def emit(self, eng, build, waits=(), sem=None, dma=False, tag=None):
        assert eng in self.ENGINES
        if sem is not None:
            self._sem(sem)
        self.items.append([eng, build, list(waits), sem, dma, tag])

    def capture(self, fn, *a, **kw):
        old = self.items
        self.items = []
        fn(*a, **kw)
        out = self.items
        self.items = old
        return out

    def merge(self, *lists):
        lists = [l for l in lists if l]
        idx = [0] * len(lists)
        while True:
            best = -1
            bestf = 2.0
            for i, l in enumerate(lists):
                if idx[i] < len(l):
                    f = idx[i] / len(l)
                    if f < bestf:
                        bestf = f
                        best = i
            if best < 0:
                break
            self.items.append(lists[best][idx[best]])
            idx[best] += 1

    def run(self, block):
        counts = {s: 0 for s in self.sems}
        events = {}
        resolved = []
        for eng, build, waits, sem, dma, tag in self.items:
            rw = []
            for w in waits:
                if isinstance(w, tuple) and w[0] == "OPT":
                    if w[1] not in events:
                        continue
                    w = w[1]
                if isinstance(w, tuple):
                    rw.append((w[0], w[1], None))
                else:
                    if w not in events:
                        raise KeyError(f"wait on unknown tag {w!r}")
                    s, v, peng, pdma = events[w]
                    if pdma:
                        rw.append((s, counts[s], None))
                    else:
                        rw.append((s, v, peng))
            val = None
            if sem is not None:
                counts[sem] += 16 if dma else 1
                val = counts[sem]
            if tag is not None:
                assert sem is not None, f"tag {tag} needs a sem"
                assert tag not in events, f"duplicate tag {tag}"
                events[tag] = (sem, val, eng, dma)
            resolved.append((eng, build, rw, sem, dma))
        self.final_counts = dict(counts)
        self.events = events

        def make(name):
            def f(e):
                last = {}
                for eng, build, rw, sem, dma in resolved:
                    if eng != name:
                        continue
                    byname = {}
                    for s, v, peng in rw:
                        if v > byname.get(s, -1):
                            byname[s] = v
                    for s, v in byname.items():
                        if last.get(s, -1) >= v:
                            continue
                        e.wait_ge(self.sems[s], v)
                        last[s] = v
                    inst = build(e)
                    if sem is not None:
                        inst.then_inc(self.sems[sem], 16 if dma else 1)
            return f

        block.tensor(make("tensor"))
        block.vector(make("vector"))
        block.scalar(make("scalar"))
        block.gpsimd(make("gpsimd"))
        block.sync(make("sync"))


def build_nc(T=4096, debug_taps=False):
    TG = 512
    NTG = T // TG
    ST = min(2048, T)
    NST = T // ST
    NTS = ST // 512
    NSC = T // 128
    NQT = TG // 128
    NTB = TG // 128
    NTI = ST // 128
    NBLK = T // 128
    NU = NTG * H

    nc = bass.Bass("TRN2", target_bir_lowering=False)

    x_h = nc.dram_tensor("x", [T, D], F32, kind="ExternalInput").ap()
    wqkv_h = nc.dram_tensor("wqkv", [3 * D, D], F32, kind="ExternalInput").ap()
    bqkv_h = nc.dram_tensor("bqkv", [3 * D], F32, kind="ExternalInput").ap()
    wo_h = nc.dram_tensor("wo", [D, D], F32, kind="ExternalInput").ap()
    bo_h = nc.dram_tensor("bo", [D], F32, kind="ExternalInput").ap()
    wq_h = nc.dram_tensor("wqg", [DH], F32, kind="ExternalInput").ap()
    wk_h = nc.dram_tensor("wkg", [DH], F32, kind="ExternalInput").ap()
    wr_h = nc.dram_tensor("wr", [D, D], F32, kind="ExternalInput").ap()
    wkf_h = nc.dram_tensor("wkf", [D, D], F32, kind="ExternalInput").ap()
    wv_h = nc.dram_tensor("wv", [D, D], F32, kind="ExternalInput").ap()
    mur_h = nc.dram_tensor("mur", [D], F32, kind="ExternalInput").ap()
    muk_h = nc.dram_tensor("muk", [D], F32, kind="ExternalInput").ap()
    out_h = nc.dram_tensor("out", [T, D], F32, kind="ExternalOutput").ap()

    ktap = "ExternalOutput" if debug_taps else "Internal"
    wTq_d = nc.dram_tensor("wTqkv", [NCC, 128, 3 * D], BF16, kind="Internal").ap()
    wTo_d = nc.dram_tensor("wTo", [NCC, 128, D], BF16, kind="Internal").ap()
    wTr_d = nc.dram_tensor("wTr", [NCC, 128, D], BF16, kind="Internal").ap()
    wTk_d = nc.dram_tensor("wTk", [NCC, 128, D], BF16, kind="Internal").ap()
    wTv_d = nc.dram_tensor("wTv", [NCC, 128, D], BF16, kind="Internal").ap()
    qsp_d = nc.dram_tensor("qsp", [NP, 128, T], BF16, kind=ktap).ap()
    ltap_d = nc.dram_tensor("ltap", [NTG, 16, TG], BF16, kind=ktap).ap()
    x1tap_d = nc.dram_tensor("x1tap", [T, D], F32, kind=ktap).ap()
    ztap_d = nc.dram_tensor("ztap", [NTG, 65, 4, TG], BF16, kind=ktap).ap()
    ksp_d = nc.dram_tensor("ksp", [NP, 128, T], BF16, kind=ktap).ap()
    vhat_d = nc.dram_tensor("vhat", [H, 128, NBLK * 65], BF16, kind=ktap).ap()
    mq_d = nc.dram_tensor("mq", [H, T], BF16, kind=ktap).ap()
    wq2_d = nc.dram_tensor("wq2", [DH], F32, kind="Internal").ap()
    wk2_d = nc.dram_tensor("wk2", [DH], F32, kind="Internal").ap()

    x_t = x_h.rearrange("(ti p) c -> p ti c", p=128)
    out_t = out_h.rearrange("(ti p) c -> p ti c", p=128)

    es = contextlib.ExitStack()
    P = Prog(nc, es)
    E = P.emit

    def sbt(name, shape, dt=BF16, stk=None):
        return (stk or es).enter_context(nc.sbuf_tensor(name, shape, dt))

    def pst(name, shape, dt=F32, stk=None):
        return (stk or es).enter_context(nc.psum_tensor(name, shape, dt))

    ident = sbt("ident", [128, 128], BF16)
    identf = sbt("identf", [128, 128], F32)
    cst = sbt("cst", [128, 8], F32)
    bq_sb = sbt("bq", [128, 24], F32)
    mu_sb = sbt("mu", [128, 16], F32)
    mufm = sbt("mufm", [128, NCC, 128])
    mkfm = sbt("mkfm", [128, NCC, 128])
    ones128 = sbt("ones128", [128, 128])
    borep = sbt("borep", [128, D])
    E2 = sbt("E2", [96, D])
    alpha8 = sbt("alpha8", [8, 1], F32)
    kmx8 = sbt("kmx8", [8, 1], F32)
    arow = sbt("arow", [1, 1], F32)
    alpham = sbt("alpham", [1, 1], F32)
    alphab = sbt("alphab", [1, 1])
    a16 = sbt("a16", [128, 1], F32)
    wq2row = sbt("wq2row", [1, DH], F32)
    wk2row = sbt("wk2row", [1, DH], F32)

    esA = contextlib.ExitStack()
    borow = sbt("borow", [1, D], F32, stk=esA)
    bvrow_bf = sbt("bvrowbf", [1, D], stk=esA)
    ones_row = sbt("onesrow", [1, 128], stk=esA)
    ones1f = sbt("ones1f", [1, 128], F32, stk=esA)
    blkq = [sbt(f"blkq_{t}", [128, 40], stk=esA) for t in range(NTS)]
    blkk = [sbt(f"blkk_{t}", [128, 40], stk=esA) for t in range(NTS)]
    msr = [sbt("msr0", [8, 512], stk=esA)]
    E8q = [sbt(f"e8q_{t}", [96, 128], stk=esA) for t in range(NTS)]
    E8k = [sbt(f"e8k_{t}", [96, 128], stk=esA) for t in range(NTS)]
    wqrow = sbt("wqrow", [1, DH], stk=esA)
    wkrow = sbt("wkrow", [1, DH], stk=esA)
    wnat = [sbt(f"wnat{i}", [128, D], stk=esA) for i in range(2)]
    wTsb = [sbt(f"wTsb{i}", [128, D], stk=esA) for i in range(2)]
    xst = [sbt(f"xst{i}", [128, 2, D], F32, stk=esA) for i in range(2)]
    u1 = [sbt(f"u1_{i}", [128, NCC, ST], stk=esA) for i in range(2)]
    u1tm = [sbt(f"u1tm{i}", [128, D], stk=esA) for i in range(2)]
    bn12 = sbt("bn12", [128, 12], F32, stk=esA)
    bnag = sbt("bnag", [128, 2], F32, stk=esA)
    nrm = [sbt(f"nrm{i}", [128, 2], F32, stk=esA) for i in range(2)]
    wpair = [sbt(f"wpair{i}", [128, 3, NCC, 128], stk=esA) for i in range(3)]
    qraw = [sbt(f"qraw{i}", [128, ST], F32, stk=esA) for i in range(4)]
    qsq = [sbt(f"qsq{i}", [128, ST], stk=esA) for i in range(4)]
    rrs = sbt("rrs", [96, 512], stk=esA)
    spl = [sbt(f"spl{i}", [128, ST], stk=esA) for i in range(4)]
    vstg = [sbt(f"vstg{i}", [128, 4, 2, 65], stk=esA) for i in range(2)]

    pq = pst("pq", [128, 1024], stk=esA)
    prms = pst("prms", [40, 512], stk=esA)
    pexp = [pst(f"pexp{i}", [128, 512], stk=esA) for i in range(2)]
    pv = pst("pv", [128, 512], stk=esA)
    pu1T = pst("pu1T", [128, 1024], BF16, stk=esA)
    pwtp = pst("pwtp", [128, 1024], BF16, stk=esA)

    E("gpsimd", lambda e: e.memset(ident[:], 0.0), sem="sgp", tag="idz")
    E("gpsimd", lambda e: e.affine_select(
        out=ident[:], in_=ident[:], pattern=[[-1, 128]],
        compare_op=ALU.not_equal, fill=1.0, base=0, channel_multiplier=1),
      waits=["idz"], sem="sgp", tag="idsel")
    E("gpsimd", lambda e: e.memset(identf[:], 0.0), sem="sgp", tag="idzf")
    E("gpsimd", lambda e: e.affine_select(
        out=identf[:], in_=identf[:], pattern=[[-1, 128]],
        compare_op=ALU.not_equal, fill=1.0, base=0, channel_multiplier=1),
      waits=["idzf"], sem="sgp", tag="idself")
    for col, v in [(0, 1e-5), (1, 1e-6), (2, -0.5), (3, 0.5), (4, 1.0),
                   (5, 1.0 / 64.0), (6, -1.0)]:
        E("gpsimd", lambda e, c=col, vv=v: e.memset(cst[:, c:c + 1], vv))
    E("gpsimd", lambda e: e.memset(ones_row[:], 1.0))
    E("gpsimd", lambda e: e.memset(ones1f[:], 1.0))
    E("gpsimd", lambda e: e.memset(ones128[:], 1.0), sem="sgp", tag="ones128d")
    E("gpsimd", lambda e: e.memset(E2[:], 0.0), sem="sgp", tag="e2z")
    for h in range(16):
        E("sync", lambda e, h=h: e.dma_start(
            out=E2[h:h + 1, DH * h:DH * (h + 1)], in_=ones128[0:1, 0:DH]),
          waits=["e2z", "ones128d"], sem="sld", dma=True, tag=f"e2r_{h}")
    for t in range(NTS):
        for nm, blkx in (("q", blkq), ("k", blkk)):
            E("gpsimd", lambda e, t=t, blkx=blkx: e.memset(blkx[t][:], 0.0),
              sem="sgp", tag=f"bkz{nm}_{t}")
            E("gpsimd", lambda e, t=t, blkx=blkx: e.memset(blkx[t][0:64, t:t + 1], 1.0),
              waits=[f"bkz{nm}_{t}"], sem="sgp", tag=f"bka{nm}_{t}")
            E("gpsimd", lambda e, t=t, blkx=blkx: e.memset(
                blkx[t][64:128, NTS + t:NTS + t + 1], 1.0),
              waits=[f"bkz{nm}_{t}"], sem="sgp", tag=f"bkb{nm}_{t}")
        E("gpsimd", lambda e, t=t: e.memset(E8q[t][:], 0.0), sem="sgp", tag=f"e8qz_{t}")
        E("gpsimd", lambda e, t=t: e.memset(E8k[t][:], 0.0), sem="sgp", tag=f"e8kz_{t}")
    E("gpsimd", lambda e: e.memset(alpha8[:], 0.0), sem="sgp", tag="alz")
    for i in range(2):
        E("gpsimd", lambda e, i=i: e.memset(vstg[i][:, :, :, 64:65], 1.0),
          sem="sgp", tag=f"vones{i}")
    E("gpsimd", lambda e: e.memset(rrs[0:96, :], 0.0), sem="sgp", tag="rrsz")
    E("gpsimd", lambda e: e.memset(cst[:, 7:8], 0.0), sem="sgp", tag="consts")

    E("sync", lambda e: e.dma_start(out=bq_sb[:], in_=bqkv_h.rearrange("(c p) -> p c", p=128), allow_slow_non_contiguous=True),
      sem="sld", dma=True, tag="ld_bq")
    E("sync", lambda e: e.dma_start(out=mu_sb[:, 0:8], in_=mur_h.rearrange("(c p) -> p c", p=128), allow_slow_non_contiguous=True),
      sem="sld", dma=True, tag="ld_mur")
    E("sync", lambda e: e.dma_start(out=mu_sb[:, 8:16], in_=muk_h.rearrange("(c p) -> p c", p=128), allow_slow_non_contiguous=True),
      sem="sld", dma=True, tag="ld_muk")
    E("sync", lambda e: e.dma_start(out=borow[:], in_=bo_h.rearrange("(o d) -> o d", o=1)),
      sem="sld", dma=True, tag="ld_bo")
    E("gpsimd", lambda e: e.dma_start(out=bvrow_bf[:], in_=bqkv_h[2 * D:3 * D].rearrange("(o d) -> o d", o=1)),
      sem="sgl", dma=True, tag="bvbf")
    E("gpsimd", lambda e: e.dma_start(out=wqrow[:], in_=wq_h.rearrange("(o d) -> o d", o=1)),
      sem="sgl", dma=True, tag="ld_wq")
    E("gpsimd", lambda e: e.dma_start(out=wkrow[:], in_=wk_h.rearrange("(o d) -> o d", o=1)),
      sem="sgl", dma=True, tag="ld_wk")
    E("vector", lambda e: e.tensor_mul(wq2row[:], wqrow[:], wqrow[:]),
      waits=["ld_wq"], sem="svv", tag="wq2v")
    E("vector", lambda e: e.tensor_mul(wk2row[:], wkrow[:], wkrow[:]),
      waits=["ld_wk"], sem="svv", tag="wk2v")
    E("sync", lambda e: e.dma_start(out=wq2_d.rearrange("(o d) -> o d", o=1), in_=wq2row[:]),
      waits=["wq2v"], sem="sld", dma=True, tag="wq2d")
    E("sync", lambda e: e.dma_start(out=wk2_d.rearrange("(o d) -> o d", o=1), in_=wk2row[:]),
      waits=["wk2v"], sem="sld", dma=True, tag="wk2d")
    for t in range(NTS):
        for nm, blkx, gsrc in (("q", blkq, wq2_d), ("k", blkk, wk2_d)):
            E("gpsimd", lambda e, t=t, blkx=blkx, gsrc=gsrc: e.dma_start(
                out=blkx[t][0:64, 32 + t:33 + t],
                in_=gsrc.rearrange("(p o) -> p o", o=1)),
              waits=["wq2d", "wk2d", f"bkz{nm}_{t}"], sem="sgl", dma=True)
            E("gpsimd", lambda e, t=t, blkx=blkx, gsrc=gsrc: e.dma_start(
                out=blkx[t][64:128, 32 + NTS + t:33 + NTS + t],
                in_=gsrc.rearrange("(p o) -> p o", o=1)),
              sem="sgl", dma=True, tag=f"blkw{nm}_{t}")
    for t in range(NTS):
        for (Ew, row, tgd) in [(E8q, wqrow, "q"), (E8k, wkrow, "k")]:
            E("sync", lambda e, Ew=Ew, t=t, row=row: e.dma_start(
                out=Ew[t][t:t + 1, 0:DH], in_=row[:]),
              waits=[f"ld_w{tgd}", "consts", f"e8{tgd}z_{t}"], sem="sld", dma=True)
            E("sync", lambda e, Ew=Ew, t=t, row=row: e.dma_start(
                out=Ew[t][NTS + t:NTS + t + 1, DH:128], in_=row[:]),
              sem="sld", dma=True, tag=f"ld_e8{tgd}_{t}")
    for cc in range(NCC):
        for (dst, col) in [(mufm, cc), (mkfm, 8 + cc)]:
            E("vector", lambda e, dst=dst, cc=cc, col=col: e.tensor_scalar_mul(
                dst[:, cc, :], ones128[:], mu_sb[:, col:col + 1]),
              waits=["ld_mur", "ld_muk", "consts", "ones128d"], sem="svv",
              tag=f"mfm_{cc}_{col}")
    for g in range(2):
        E("tensor", lambda e, g=g: e.matmul(
            pq[:, 512 * g:512 * (g + 1)], ones1f[:],
            borow[0:1, 512 * g:512 * (g + 1)], start=True, stop=True),
          waits=["ld_bo", "consts"], sem="stm", tag=f"bormm_{g}")
        E("vector", lambda e, g=g: e.tensor_copy(
            borep[:, 512 * g:512 * (g + 1)], pq[:, 512 * g:512 * (g + 1)]),
          waits=[f"bormm_{g}"], sem="svv", tag=f"borev_{g}")

    wsrc = [(wqkv_h, wTq_d, 3 * NCC), (wo_h, wTo_d, NCC), (wr_h, wTr_d, NCC),
            (wkf_h, wTk_d, NCC), (wv_h, wTv_d, NCC)]
    wrg = []
    for src, dstT, nrg in wsrc:
        for rg in range(nrg):
            wrg.append((src, dstT, rg))
    NW = len(wrg)
    NWQ = 3 * NCC

    def emit_w(g0, g1):
        for gi in range(g0, g1):
            src, dstT, rg = wrg[gi]
            b = gi % 2
            E("gpsimd", lambda e, src=src, rg=rg, b=b: e.dma_start(
                out=wnat[b][:], in_=src[128 * rg:128 * (rg + 1), :]),
              waits=(["consts"] if gi < 2 else [f"wtr_{gi - 2}"]),
              sem=f"swl{gi % 2}", dma=True, tag=f"wld_{gi}")
            tw = [f"wld_{gi}"] + ([f"wev_{gi - 1}"] if gi >= 1 else ["borev_1"])
            for blk in range(NCC):
                E("tensor", lambda e, b=b, blk=blk: e.transpose(
                    pwtp[:, 128 * blk:128 * (blk + 1)],
                    wnat[b][:, 128 * blk:128 * (blk + 1)], ident[:]),
                  waits=tw if blk == 0 else [],
                  sem="stm", tag=(f"wtr_{gi}" if blk == NCC - 1 else None))
            E("vector", lambda e, b=b: e.tensor_copy(wTsb[b][:], pwtp[:]),
              waits=[f"wtr_{gi}"] + ([f"wst_{gi - 2}"] if gi >= 2 else []),
              sem="svv", tag=f"wev_{gi}")
            E("sync", lambda e, dstT=dstT, rg=rg, b=b: e.dma_start(
                out=dstT[:, :, 128 * rg:128 * (rg + 1)].rearrange("c p k -> p c k"),
                in_=wTsb[b][:].rearrange("p (b k) -> p b k", k=128)),
              waits=[f"wev_{gi}"], sem=f"sws{gi % 2}", dma=True, tag=f"wst_{gi}")

    emit_w(0, NWQ)
    wrest = P.capture(emit_w, NWQ, NW)

    NGG = T // 512

    def emit_a1(st):
        for g in range(ST // 256):
            gg = st * (ST // 256) + g
            gb = gg % 2
            xw = [f"u1n_{gg - 2}_1"] if gg >= 2 else []
            E("sync", lambda e, st=st, g=g, gb=gb: e.dma_start(
                out=xst[gb][:], in_=x_t[:, st * NTI + 2 * g: st * NTI + 2 * (g + 1), :]),
              waits=xw, sem=f"sxl{gg % 2}", dma=True, tag=f"xld_{gg}")
            for j in range(2):
                ti = 2 * g + j
                tj = 2 * gg + j
                nb = tj % 2
                ub = tj % 2
                E("vector", lambda e, gb=gb, j=j: e.bn_stats(bn12[:, 0:6], xst[gb][:, j, 0:512]),
                  waits=[f"xld_{gg}"] + ([f"nrmd_{tj - 1}"] if tj >= 1 else []),
                  sem="svv", tag=f"bns1_{tj}")
                E("vector", lambda e, gb=gb, j=j: e.bn_stats(bn12[:, 6:12], xst[gb][:, j, 512:1024]),
                  sem="svv", tag=f"bns2_{tj}")
                E("vector", lambda e: e.bn_aggr(bnag[:], bn12[:]),
                  waits=[f"bns1_{tj}", f"bns2_{tj}"], sem="svv", tag=f"bn_{tj}")
                E("scalar", lambda e, nb=nb: e.activation(nrm[nb][:, 1:2], bnag[:, 1:2],
                                                          AF.Ln, bias=cst[:, 0:1]),
                  waits=[f"bn_{tj}"] + ([OPT(f"u1n_{(tj - 2) // 2}_{(tj - 2) % 2}")] if tj >= 2 else []),
                  sem="ssc", tag=f"nrl_{tj}")
                E("scalar", lambda e, nb=nb: e.activation(nrm[nb][:, 1:2], nrm[nb][:, 1:2],
                                                          AF.Exp, scale=cst[:, 2:3]),
                  waits=[f"nrl_{tj}"], sem="ssc", tag=f"nre_{tj}")
                E("scalar", lambda e, nb=nb: e.copy(nrm[nb][:, 0:1], bnag[:, 0:1]),
                  waits=[f"nre_{tj}"], sem="ssc", tag=f"nrmd_{tj}")
                u1w = [f"nrmd_{tj}"]
                if tj >= 2:
                    u1w.append(f"u1tr_{tj - 2}")
                E("vector", lambda e, gb=gb, j=j, nb=nb, ub=ub: e.tensor_scalar(
                    u1tm[ub][:], xst[gb][:, j, :], nrm[nb][:, 0:1], nrm[nb][:, 1:2],
                    ALU.subtract, ALU.mult),
                  waits=u1w, sem="svv", tag=f"u1n_{gg}_{j}")
                trw = [f"u1n_{gg}_{j}"]
                if tj >= 1:
                    trw.append(f"u1ev_{tj - 1}")
                if st >= 2 and ti == 0:
                    trw.append(f"a2v_{(st - 1) * NP - 1}_{NTS - 1}")
                for blk in range(NCC):
                    E("tensor", lambda e, ub=ub, blk=blk: e.transpose(
                        pu1T[:, 128 * blk:128 * (blk + 1)],
                        u1tm[ub][:, 128 * blk:128 * (blk + 1)], ident[:]),
                      waits=trw if blk == 0 else [],
                      sem="stm", tag=(f"u1tr_{tj}" if blk == NCC - 1 else None))
                E("vector", lambda e, ti=ti, st=st: e.tensor_copy(
                    u1[st % 2][:, :, 128 * ti:128 * (ti + 1)],
                    pu1T[:].rearrange("p (c k) -> p c k", k=128)),
                  waits=[f"u1tr_{tj}"], sem="svv", tag=f"u1ev_{tj}")
    def emit_wpl(pi):
        p = pi % NP
        pb = pi % 3
        for part3 in range(3):
            ww = [f"wst_{NWQ - 1}", f"wst_{NWQ - 2}"]
            if pi >= 3:
                ww.append(f"a2v_{pi - 3}_{NTS - 1}")
            eng = "sync" if part3 == 0 else "scalar"
            sem = f"swp{pi % 3}" if part3 == 0 else f"swq{pi % 3}"
            E(eng, lambda e, p=p, part3=part3, pb=pb: e.dma_start(
                out=wpair[pb][:, part3, :, :],
                in_=wTq_d[:, :, D * part3 + 128 * p: D * part3 + 128 * (p + 1)].rearrange(
                    "c p k -> p c k")),
              waits=ww, sem=sem, dma=True, tag=f"wpl_{pi}_{part3}")

    def emit_pair_mms(st, p):
        u1tag = f"u1ev_{(st + 1) * NTI - 1}"
        if True:
            pi = st * NP + p
            pb = pi % 3
            for part in range(2):
                sn = "q" if part == 0 else "k"
                bci = p if part == 0 else NP + p
                ps_seq = pi * 2 + part
                for t in range(NTS):
                    ph = (ps_seq * NTS + t) % 2
                    qs = ps_seq * NTS + t
                    mmw = [f"wpl_{pi}_{part}", u1tag]
                    if qs < 2:
                        mmw.append("borev_1")
                    if qs >= 2:
                        pq2 = qs - 2
                        pp, tt2 = divmod(pq2, NTS)
                        ppi, ppart = divmod(pp, 2)
                        mmw.append(f"a2sq_{ppi}_{'q' if ppart == 0 else 'k'}_{tt2}")
                    for cc in range(NCC):
                        E("tensor", lambda e, pb=pb, part=part, cc=cc, t=t, ph=ph, st=st: e.matmul(
                            pq[:, 512 * ph:512 * (ph + 1)], wpair[pb][:, part, cc, :],
                            u1[st % 2][:, cc, 512 * t:512 * (t + 1)],
                            start=(cc == 0), stop=(cc == NCC - 1)),
                          waits=mmw if cc == 0 else [],
                          sem="stm", tag=(f"a2mm_{pi}_{sn}_{t}" if cc == NCC - 1 else None))
                    bsel = 2 * (pi % 2) + part
                    E("vector", lambda e, bsel=bsel, t=t, ph=ph, bci=bci: e.tensor_scalar(
                        qraw[bsel][:, 512 * t:512 * (t + 1)], pq[:, 512 * ph:512 * (ph + 1)],
                        bq_sb[:, bci:bci + 1], None, ALU.add),
                      waits=[f"a2mm_{pi}_{sn}_{t}"] +
                            ([OPT(f"a2n_{pi - 2}_{sn}_{t}")] if pi >= 2 else []),
                      sem="svv", tag=f"a2raw_{pi}_{sn}_{t}")
                    E("vector", lambda e, bsel=bsel, t=t: e.tensor_mul(
                        qsq[bsel][:, 512 * t:512 * (t + 1)],
                        qraw[bsel][:, 512 * t:512 * (t + 1)],
                        qraw[bsel][:, 512 * t:512 * (t + 1)]),
                      waits=[f"a2raw_{pi}_{sn}_{t}"] +
                            ([OPT(f"a2rmm_{pi - 2}_{sn}")] if (t == 0 and pi >= 2) else []),
                      sem="svv", tag=f"a2sq_{pi}_{sn}_{t}")

            for tbg in range(NTS):
                vs = pi * NTS + tbg
                vb = vs % 2
                vw = [f"wpl_{pi}_2", u1tag, "bvbf"]
                if vs >= 1:
                    pv2 = vs - 1
                    vw.append(f"a2vev_{pv2 // NTS}_{pv2 % NTS}")
                for tb in range(4):
                    tt = 4 * tbg + tb
                    for cc in range(NCC):
                        E("tensor", lambda e, pb=pb, cc=cc, tt=tt, tb=tb, st=st: e.matmul(
                            pv[:, 128 * tb:128 * (tb + 1)],
                            u1[st % 2][:, cc, 128 * tt:128 * (tt + 1)], wpair[pb][:, 2, cc, :],
                            start=(cc == 0), stop=False),
                          waits=vw if (tb == 0 and cc == 0) else [],
                          sem="stm")
                    E("tensor", lambda e, p=p, tb=tb: e.matmul(
                        pv[:, 128 * tb:128 * (tb + 1)],
                        ones_row[:], bvrow_bf[:, 128 * p:128 * (p + 1)],
                        start=False, stop=True),
                      sem="stm", tag=(f"a2v_{pi}_{tbg}" if tb == 3 else None))
                evw = [f"a2v_{pi}_{tbg}", f"vones{vb}"]
                if vs >= 2:
                    pv2 = vs - 2
                    evw.append(f"a2vd_{pv2 // NTS}_{pv2 % NTS}")
                E("vector", lambda e, vb=vb: e.tensor_copy(
                    vstg[vb][:, :, :, 0:64],
                    pv[:].rearrange("p (a h k) -> p a h k", a=4, h=2)),
                  waits=evw, sem="svv", tag=f"a2vev_{pi}_{tbg}")
                for hh in range(2):
                    sc0 = st * NTI + tbg * 4
                    E("sync", lambda e, p=p, hh=hh, sc0=sc0, vb=vb: e.dma_start(
                        out=vhat_d[2 * p + hh, :, 65 * sc0: 65 * (sc0 + 4)].rearrange(
                            "p (a k) -> p a k", k=65),
                        in_=vstg[vb][:, :, hh, :]),
                      waits=[f"a2vev_{pi}_{tbg}"] if hh == 0 else [],
                      sem="svd", dma=True,
                      tag=(f"a2vd_{pi}_{tbg}" if hh == 1 else None))
            if pi + 3 < NST * NP:
                emit_wpl(pi + 3)



    def emit_pair_chains(st, p):
        pi = st * NP + p
        u1tag = f"u1ev_{(st + 1) * NTI - 1}"
        NR = 2 * NTS
        for part in range(2):
            sn = "q" if part == 0 else "k"
            spd = qsp_d if part == 0 else ksp_d
            E8 = E8q if part == 0 else E8k
            blkx = blkq if part == 0 else blkk
            sbsp = spl[2 * (pi % 2) + part]
            ps_seq = pi * 2 + part
            mb = 0
            for t in range(NTS):
                rmw = [f"a2sq_{pi}_{sn}_{t}"]
                if t == 0:
                    if ps_seq >= 1:
                        pp2, ppart2 = divmod(ps_seq - 1, 2)
                        rmw.append(f"a2ml_{pp2}_{'q' if ppart2 == 0 else 'k'}")
                    if pi == 0:
                        for t2 in range(NTS):
                            rmw += [f"bka{sn}_{t2}", f"bkb{sn}_{t2}", f"blkw{sn}_{t2}"]
                bsel = 2 * (pi % 2) + part
                E("tensor", lambda e, bsel=bsel, t=t, blkx=blkx: e.matmul(
                    prms[:], blkx[t][:], qsq[bsel][:, 512 * t:512 * (t + 1)],
                    start=(t == 0), stop=(t == NTS - 1)),
                  waits=rmw, sem="stm",
                  tag=(f"a2rmm_{pi}_{sn}" if t == NTS - 1 else None))
            rlw = [f"a2rmm_{pi}_{sn}"]
            if ps_seq >= 1:
                pp2, ppart2 = divmod(ps_seq - 1, 2)
                rlw.append(f"a2mv_{pp2}_{'q' if ppart2 == 0 else 'k'}")
            E("scalar", lambda e: e.activation(rrs[0:8, :], prms[0:8, :], AF.Ln,
                                               bias=cst[0:8, 1:2], scale=cst[0:8, 5:6]),
              waits=rlw, sem="ssc", tag=f"a2rl_{pi}_{sn}")
            E("scalar", lambda e: e.activation(rrs[0:8, :], rrs[0:8, :], AF.Exp, scale=cst[0:8, 2:3]),
              waits=[f"a2rl_{pi}_{sn}"], sem="ssc", tag=f"a2rr_{pi}_{sn}")
            mlw = [f"a2rr_{pi}_{sn}"]
            if ps_seq >= 1:
                pp2, ppart2 = divmod(ps_seq - 1, 2)
                mlw.append(f"a2mq_{pp2}" if ppart2 == 0 else f"a2mr_{pp2}")
            E("scalar", lambda e, mb=mb: e.activation(msr[mb][0:NR, :], prms[32:32 + NR, :],
                                                      AF.Ln, bias=cst[0:NR, 1:2]),
              waits=mlw, sem="ssc", tag=f"a2ml_{pi}_{sn}")
            E("scalar", lambda e, mb=mb: e.activation(msr[mb][0:NR, :], msr[mb][0:NR, :],
                                                      AF.Exp, scale=cst[0:NR, 3:4]),
              waits=[f"a2ml_{pi}_{sn}"], sem="ssc", tag=f"a2ms_{pi}_{sn}")
            E("vector", lambda e, mb=mb: e.tensor_mul(msr[mb][0:NR, :], msr[mb][0:NR, :],
                                                      rrs[0:NR, :]),
              waits=[f"a2ms_{pi}_{sn}"], sem="svv", tag=f"a2mv_{pi}_{sn}")
            if part == 0:
                E("sync", lambda e, mb=mb, p=p, st=st: e.dma_start(
                    out=mq_d[2 * p, ST * st:ST * (st + 1)].rearrange(
                        "(t c) -> t c", c=512),
                    in_=msr[mb][0:NTS, :]),
                  waits=[f"a2mv_{pi}_{sn}"], sem="smq", dma=True)
                E("sync", lambda e, mb=mb, p=p, st=st: e.dma_start(
                    out=mq_d[2 * p + 1, ST * st:ST * (st + 1)].rearrange(
                        "(t c) -> t c", c=512),
                    in_=msr[mb][NTS:NR, :]),
                  sem="smq", dma=True, tag=f"a2mq_{pi}")
            else:
                krw = [f"a2mv_{pi}_{sn}"]
                if pi >= 1:
                    krw.append(f"a2am_{pi - 1}")
                E("vector", lambda e, mb=mb: e.reduce_max(kmx8[0:NR, :], msr[mb][0:NR, :],
                                                          axis=AX),
                  waits=krw, sem="svv", tag=f"a2mr_{pi}")
                E("vector", lambda e: e.tensor_max(alpha8[0:NR, :], kmx8[0:NR, :],
                                                   alpha8[0:NR, :]),
                  waits=[f"a2mr_{pi}", "alz" if pi == 0 else f"a2am_{pi - 1}"],
                  sem="svv", tag=f"a2am_{pi}")
            for t in range(NTS):
                em = ps_seq * NTS + t
                emw = [f"a2rr_{pi}_{sn}", f"ld_e8{sn}_{t}"] + (["rrsz"] if pi == 0 else [])
                if em >= 2:
                    pe, te = divmod(em - 2, NTS)
                    pei, pep = divmod(pe, 2)
                    emw.append(f"a2n_{pei}_{'q' if pep == 0 else 'k'}_{te}")
                E("tensor", lambda e, E8=E8, t=t, em=em: e.matmul(
                    pexp[em % 2][:], E8[t][:], rrs[0:96, :], start=True, stop=True),
                  waits=emw, sem="stm", tag=f"a2emm_{pi}_{sn}_{t}")
                nw = [f"a2emm_{pi}_{sn}_{t}", f"a2raw_{pi}_{sn}_{t}"]
                if t == 0 and pi >= 2:
                    nw.append(f"a2sp_{pi - 2}_{sn}")
                bsel = 2 * (pi % 2) + part
                E("vector", lambda e, bsel=bsel, sbsp=sbsp, t=t, em=em: e.tensor_mul(
                    sbsp[:, 512 * t:512 * (t + 1)], qraw[bsel][:, 512 * t:512 * (t + 1)],
                    pexp[em % 2][:]),
                  waits=nw, sem="svv", tag=f"a2n_{pi}_{sn}_{t}")
            E("sync", lambda e, spd=spd, p=p, st=st, sbsp=sbsp: e.dma_start(
                out=spd[p, :, ST * st:ST * (st + 1)], in_=sbsp[:]),
              waits=[f"a2n_{pi}_{sn}_{NTS - 1}"], sem="ssp", dma=True,
              tag=f"a2sp_{pi}_{sn}")


    prev_ch = []
    NPAIR = NST * NP
    for pi0 in range(min(3, NPAIR)):
        emit_wpl(pi0)
    a1next = []
    for st in range(NST):
        if st == 0:
            P.merge(P.capture(emit_a1, 0))
        for p in range(NP):
            pi = st * NP + p
            if p == NP - 3 and st + 1 < NST:
                a1next = P.capture(emit_a1, st + 1)
            extra = []
            if a1next:
                n3 = (len(a1next) + NP - 1 - p) // (NP - p) if p >= NP - 3 else 0
                extra = a1next[:n3]
                a1next = a1next[n3:]
            mm = P.capture(emit_pair_mms, st, p)
            wchunk = wrest[:22]
            wrest = wrest[22:]
            P.merge(prev_ch, mm, extra, wchunk)
            prev_ch = P.capture(emit_pair_chains, st, p)
    P.merge(prev_ch, wrest)

    lastpi = NST * NP - 1
    E("sync", lambda e: e.nop(),
      waits=[f"a2sp_{lastpi}_k", f"a2vd_{lastpi}_{NTS - 1}", f"wst_{NW - 1}",
             f"wst_{NW - 2}", f"a2mq_{lastpi}"],
      sem="sad", tag="a2alldma")

    lastA = {
        "tensor": [f"a2v_{lastpi}_{NTS - 1}"],
        "vector": [f"a2vev_{lastpi}_{NTS - 1}", f"a2n_{lastpi}_k_{NTS - 1}",
                   f"a2am_{lastpi}"],
        "scalar": [f"a2ms_{lastpi}_k"],
        "gpsimd": [f"wld_{NW - 1}"],
        "sync": ["a2alldma"],
    }
    for eng in P.ENGINES:
        E(eng, lambda e: e.nop(), waits=lastA[eng], sem="sbar", tag=f"barA_{eng}")
    barA = [f"barA_{e}" for e in P.ENGINES]

    esA.close()

    esB = contextlib.ExitStack()
    woT = sbt("woT", [128, NCC, D], stk=esB)
    wrT = sbt("wrT", [128, NCC, D], stk=esB)
    wkT = sbt("wkT", [128, NCC, D], stk=esB)
    wvT = sbt("wvT", [128, NCC, D], stk=esB)
    ktil = [sbt(f"ktil{i}", [128, T], stk=esB) for i in range(3)]
    ksmp = [sbt(f"ksmp{i}", [128, 128], stk=esB) for i in range(3)]
    mqr = [sbt(f"mqr{i}", [4, 128], stk=esB) for i in range(3)]
    mrow = [sbt(f"mrow{i}", [4, 128], stk=esB) for i in range(2)]
    mneg4 = sbt("mneg4", [128, NQT], F32, stk=esB)
    negkr = sbt("negkr", [4, 128], F32, stk=esB)
    qtil = [sbt(f"qtil{i}", [128, TG], stk=esB) for i in range(3)]
    PT = [sbt(f"PT{i}", [128, 1024], stk=esB) for i in range(2)]
    vstr = [sbt(f"vstr{i}", [128, NBLK * 65], stk=esB) for i in range(3)]
    zstg = sbt("zstg", [65, 4, TG], stk=esB)
    lgat = [sbt(f"lgat{i}", [16, TG], stk=esB) for i in range(2)]
    lrec = sbt("lrec", [96, TG], stk=esB)
    ztg = [sbt(f"ztg{i}", [128, NCC, TG], stk=esB) for i in range(2)]
    xre = sbt("xre", [128, 2, D], stk=esB)
    x1 = sbt("x1", [128, NTB, D], F32, stk=esB)
    bn12b = sbt("bn12b", [128, 12], F32, stk=esB)
    bnagb = sbt("bnagb", [128, 2], F32, stk=esB)
    nrm2 = [sbt(f"nrm2{i}", [128, 2], F32, stk=esB) for i in range(2)]
    u2tm = [sbt(f"u2tm{i}", [128, D], F32, stk=esB) for i in range(2)]
    u2fm = sbt("u2fm", [128, NCC, TG], stk=esB)
    bcol = sbt("bcol", [128, NCC, 1], stk=esB)
    xr = sbt("xr", [128, NCC, TG], stk=esB)
    xk = sbt("xk", [128, NCC, TG], stk=esB)
    tmpv = sbt("tmpv", [128, NCC, 128], stk=esB)
    tmpg = sbt("tmpg", [128, NCC, 128], stk=esB)
    th = u2fm
    krelu = [sbt("krelu0", [128, 512], stk=esB)]
    kact = xr
    t1f = [sbt(f"t1f{i}", [128, 512], stk=esB) for i in range(2)]
    t2f = [sbt(f"t2f{i}", [128, 512], stk=esB) for i in range(2)]

    P2 = pst("P2", [128, 2048], stk=esB)
    zps = pst("zps", [65, TG], stk=esB)
    pms = pst("pms", [128, 512], stk=esB)
    tl = [pst(f"tl{i}", [128, 512], stk=esB) for i in range(2)]

    tlst = {0: None, 1: None}
    tlrr = [0]

    def tl_use(extra):
        par = tlrr[0]
        tlrr[0] = 1 - par
        w = list(extra)
        if tlst[par] is not None:
            w.append(tlst[par])
        return par, w

    def tl_done(par, tag):
        tlst[par] = tag

    for i in range(3):
        E("gpsimd", lambda e, i=i: e.memset(ktil[i][64:128, :], 0.0), waits=barA,
          sem="sgp", tag=f"kz_{i}")
        E("gpsimd", lambda e, i=i: e.memset(ktil[i][64:65, :], 1.0), waits=[f"kz_{i}"],
          sem="sgp", tag=f"k1_{i}")
        E("gpsimd", lambda e, i=i: e.memset(qtil[i][64:128, :], 0.0), sem="sgp",
          tag=f"qz_{i}")
        E("gpsimd", lambda e, i=i: e.memset(ksmp[i][64:128, :], 0.0), sem="sgp",
          tag=f"ksz_{i}")
    E("vector", lambda e: e.memset(bcol[:], 0.0), waits=barA, sem="svv",
      tag="bnd_init")
    E("vector", lambda e: e.memset(lrec[0:96, :], 0.0), waits=barA, sem="svv",
      tag="lrec_init")
    bwl_srcs = [(woT, wTo_d), (wrT, wTr_d), (wkT, wTk_d), (wvT, wTv_d)]
    bwl_ix = [0]

    def emit_bwl(nchunks):
        for _ in range(nchunks):
            ci = bwl_ix[0]
            if ci >= 4 * NCC:
                return
            bwl_ix[0] = ci + 1
            wn, cc = divmod(ci, NCC)
            wsb, wd = bwl_srcs[wn]
            E("sync", lambda e, wsb=wsb, wd=wd, cc=cc: e.dma_start(
                out=wsb[:, cc, :], in_=wd[cc, :, :]),
              waits=barA, sem="swl2", dma=True,
              tag=(f"bwl_{wn}" if cc == NCC - 1 else None))

    E("tensor", lambda e: e.transpose(tl[0][0:1, 0:8], alpha8[0:8, :],
                                      identf[0:8, 0:8]),
      waits=barA, sem="stm", tag="batr")
    E("vector", lambda e: e.reduce_max(arow[:], tl[0][0:1, 0:8], axis=AX),
      waits=["batr"], sem="svv", tag="bamx")
    E("scalar", lambda e: e.activation(alpham[:], arow[:], AF.Identity,
                                       scale=cst[0:1, 6:7]),
      waits=["bamx"], sem="ssc", tag="basc")
    E("vector", lambda e: e.tensor_copy(alphab[:], alpham[:]), waits=["basc"],
      sem="svv", tag="babf")
    E("tensor", lambda e: e.matmul(tl[0][:, 0:1], ones128[0:1, :], alphab[:],
                                   start=True, stop=True),
      waits=["babf"], sem="stm", tag="babc")
    E("scalar", lambda e: e.copy(a16[:], tl[0][:, 0:1]), waits=["babc"], sem="ssc",
      tag="ba16")
    tlst[0] = "ba16"

    def kql(u, waits):
        ub = u % 3
        tg, h = divmod(u, H)
        E("sync", lambda e, h=h, ub=ub: e.dma_start(
            out=ktil[ub][0:64, :], in_=ksp_d[h // 2, 64 * (h % 2):64 * (h % 2) + 64, :]),
          waits=waits, sem=f"skl{ub}", dma=True, tag=f"bkl_{u}")
        E("sync", lambda e, h=h, tg=tg, ub=ub: e.dma_start(
            out=qtil[ub][0:64, :],
            in_=qsp_d[h // 2, 64 * (h % 2):64 * (h % 2) + 64, TG * tg:TG * (tg + 1)]),
          sem=f"skl{ub}", dma=True, tag=f"bql_{u}")
        c0 = (T // NTG) * ((tg + 3) % NTG) + (128 if T > 256 else 0)
        E("sync", lambda e, h=h, tg=tg, ub=ub: e.dma_start(
            out=mqr[ub][0:NQT, :],
            in_=mq_d[h, TG * tg:TG * (tg + 1)].rearrange("(q c) -> q c", c=128)),
          sem=f"skl{ub}", dma=True, tag=f"bmc_{u}")
        E("sync", lambda e, h=h, ub=ub, c0=c0: e.dma_start(
            out=ksmp[ub][0:64, :],
            in_=ksp_d[h // 2, 64 * (h % 2):64 * (h % 2) + 64, c0:c0 + 128]),
          waits=(waits if u < 3 else []), sem=f"skl{ub}", dma=True, tag=f"bkls_{u}")
        E("sync", lambda e, h=h, ub=ub: e.dma_start(out=vstr[ub][:], in_=vhat_d[h, :, :]),
          sem=f"skl{ub}", dma=True, tag=f"bvl_{u}")

    kql(0, barA + ["qz_0"])
    if NU > 1:
        kql(1, ["qz_1"])
    if NU > 2:
        kql(2, ["qz_2"])

    def pv_emit(u, sc, ub):
        w = [f"bexp_{u}_{sc}"]
        if sc == 0:
            w.append(f"bvl_{u}")
            if u >= 1:
                w.append(f"bzev_{u - 1}")
        E("tensor", lambda e, u=u, ub=ub, sc=sc: e.matmul(
            zps[:], vstr[ub][:, 65 * sc:65 * (sc + 1)],
            PT[(sc // 2) % 2][:, 512 * (sc % 2):512 * (sc % 2 + 1)],
            start=(sc == 0), stop=(sc == NSC - 1)),
          waits=w, sem="stm", tag=f"bpv_{u}_{sc}")

    def mchain(u):
        tg, h = divmod(u, H)
        ub = u % 3
        w0 = [f"bql_{u}", f"bkls_{u}"]
        if u < 3:
            w0 += [f"qz_{ub}", f"ksz_{ub}"]
        if u >= 1:
            w0.append(f"bmcp_{u - 1}")
        for qt in range(NQT):
            E("tensor", lambda e, ub=ub, qt=qt: e.matmul(
                pms[:, 128 * qt:128 * (qt + 1)],
                qtil[ub][0:128, 128 * qt:128 * (qt + 1)], ksmp[ub][0:128, :],
                start=True, stop=True),
              waits=w0 if qt == 0 else [], sem="stm", tag=f"bsm_{u}_{qt}")
        for qt in range(NQT):
            rw = [f"bsm_{u}_{NQT - 1}"]
            if qt == 0 and u >= 1:
                rw.append(f"bmtr_{u - 1}")
            E("vector", lambda e, qt=qt: e.reduce_max(
                mneg4[:, qt:qt + 1], pms[:, 128 * qt:128 * (qt + 1)],
                axis=AX, negate=True),
              waits=rw, sem="svv", tag=f"bsr_{u}_{qt}")
        kw = [f"bmc_{u}", "ba16"]
        if u >= 1:
            kw.append(f"bmcp_{u - 1}")
        E("vector", lambda e, ub=ub: e.tensor_scalar(
            negkr[:], mqr[ub][0:NQT, :], a16[0:4, 0:1], 70.0, ALU.mult, ALU.add),
          waits=kw, sem="svv", tag=f"bnk_{u}")
        E("tensor", lambda e: e.transpose(pms[0:NQT, 0:128],
                                          mneg4[:, 0:NQT], identf[:]),
          waits=[f"bsr_{u}_{NQT - 1}"], sem="stm", tag=f"bmtr_{u}")
        cw = [f"bmtr_{u}", f"bnk_{u}"]
        if u >= 2:
            cw.append(f"bmd_{u - 2}")
        E("vector", lambda e, u=u: e.tensor_tensor(
            mrow[u % 2][:], pms[0:NQT, 0:128], negkr[:], ALU.min),
          waits=cw, sem="svv", tag=f"bmcp_{u}")
        E("scalar", lambda e, u=u, ub=ub: e.dma_start(
            out=qtil[ub][64:65, :], in_=mrow[u % 2][0:NQT, :]),
          waits=[f"bmcp_{u}"] + ([f"qz_{ub}"] if u < 3 else []),
          sem="smd", dma=True, tag=f"bmd_{u}")

    def zev_block(u):
        tg, h = divmod(u, H)
        slot = h % 4
        zw = [f"bpv_{u}_{NSC - 1}"]
        if u >= 4:
            zw.append(f"bzrp_{u - 4}")
            zw.append(f"blg_{(u - 4) // 4}")
        E("vector", lambda e, slot=slot: e.tensor_copy(zstg[:, slot, :], zps[:]),
          waits=zw, sem="svv", tag=f"bzev_{u}")
        rw = [f"bzev_{u}"]
        if tg >= 2:
            rw.append(f"bwom_{tg - 2}_7")
        E("sync", lambda e, h=h, tg=tg, slot=slot: e.dma_start(
            out=ztg[tg % 2][64 * (h % 2):64 * (h % 2) + 64, h // 2, :],
            in_=zstg[0:64, slot, :]),
          waits=rw, sem="szr", dma=True, tag=f"bzrp_{u}")
        if h % 4 == 3:
            hf = h // 4
            lw = [f"bzev_{u}"]
            if tg >= 2:
                lw.append(f"blrc_{tg - 2}")
            E("sync", lambda e, hf=hf, tg=tg: e.dma_start(
                out=lgat[tg % 2][4 * hf:4 * (hf + 1), :],
                in_=zstg[64:65, :, :]),
              waits=lw, sem="slg", dma=True, tag=f"blg_{u // 4}")

    def unit_carry(up):
        ubp = up % 3
        pv_emit(up, NSC - 2, ubp)
        pv_emit(up, NSC - 1, ubp)
        zev_block(up)
        if up + 3 < NU:
            kql(up + 3, [f"bpv_{up}_{NSC - 1}"])
        if up + 2 < NU:
            mchain(up + 2)
        if up >= 1:
            emit_bwl(3)

    def unit_p2(u):
        tg, h = divmod(u, H)
        ub = u % 3
        for sc in range(NSC):
            bank = sc % 4
            w = []
            if sc == 0:
                w = [f"bmd_{u}"]
            j4 = sc - 4
            if j4 >= 0:
                w.append(f"bexp_{u}_{j4}")
            elif u >= 1:
                w.append(f"bexp_{u - 1}_{NSC - 4 + sc}")
            E("tensor", lambda e, ub=ub, sc=sc, bank=bank: e.matmul(
                P2[:, 512 * bank:512 * (bank + 1)],
                ktil[ub][0:128, 128 * sc:128 * (sc + 1)], qtil[ub][0:128, :],
                start=True, stop=True),
              waits=w, sem="stm", tag=f"bp2m_{u}_{sc}")
            if sc == 2 and u >= 1:
                unit_carry(u - 1)
            ew = [f"bp2m_{u}_{sc}"]
            if sc >= 4:
                ew.append(f"bpv_{u}_{sc - 4}")
            elif u >= 1:
                ew.append(f"bpv_{u - 1}_{NSC - 4 + sc}")
            E("scalar", lambda e, ub=ub, sc=sc, bank=bank: e.activation(
                PT[(sc // 2) % 2][:, 512 * (sc % 2):512 * (sc % 2 + 1)],
                P2[:, 512 * bank:512 * (bank + 1)], AF.Exp),
              waits=ew, sem="sse", tag=f"bexp_{u}_{sc}")
            if sc >= 2:
                pv_emit(u, sc - 2, ub)

    def tail(tg):
        if debug_taps:
            E("sync", lambda e, tg=tg: e.dma_start(out=ltap_d[tg], in_=lgat[tg % 2][:]),
              waits=[f"blg_{tg * 4 + q}" for q in range(4)], sem="slg", dma=True)
            E("sync", lambda e, tg=tg: e.dma_start(out=ztap_d[tg], in_=zstg[:]),
              waits=[f"blg_{tg * 4 + q}" for q in range(4)], sem="slg", dma=True)
        def _lrec(e):
            with nc.allow_low_precision(reason="1/l stored bf16 (same as prior path)"):
                return e.reciprocal(lrec[0:16, :], lgat[tg % 2][:])
        E("vector", _lrec,
          waits=[f"blg_{tg * 4 + q}" for q in range(4)] +
                ([OPT(f"blrm_{tg - 1}_{NCC - 1}")] if tg >= 1 else []),
          sem="svv", tag=f"blrc_{tg}")
        for j in range(NCC):
            par, w = tl_use([f"blrc_{tg}", f"bzrp_{tg * H + 2 * j}",
                             f"bzrp_{tg * H + 2 * j + 1}"] +
                            (["lrec_init"] if tg == 0 and j == 0 else []))
            E("tensor", lambda e, j=j, par=par: e.matmul(
                tl[par][:], E2[0:96, 128 * j:128 * (j + 1)], lrec[0:96, 0:TG],
                start=True, stop=True),
              waits=w, sem="stm", tag=f"blrm_{tg}_{j}")
            E("vector", lambda e, j=j, par=par, tg=tg: e.tensor_mul(
                ztg[tg % 2][:, j, :], ztg[tg % 2][:, j, :], tl[par][:]),
              waits=[f"blrm_{tg}_{j}"], sem="svv", tag=f"bzn_{tg}_{j}")
            tl_done(par, f"bzn_{tg}_{j}")
        def xre_load(tb2):
            gtb2 = tg * NTB + tb2
            w = []
            if tb2 >= 2:
                w += [f"bx1_{tg}_{2 * (tb2 - 2)}", f"bx1_{tg}_{2 * (tb2 - 2) + 1}"]
            elif tg >= 1:
                w += [OPT(f"bx1_{tg - 1}_{2 * (tb2 + 2)}"), OPT(f"bx1_{tg - 1}_{2 * (tb2 + 2) + 1}")]
            E("gpsimd", lambda e, tg=tg, tb2=tb2: e.dma_start(
                out=xre[:, tb2 % 2, :], in_=x_t[:, tg * NTB + tb2, :]),
              waits=w, sem=f"sxr{tb2 % 2}", dma=True, tag=f"bxl_{gtb2}")

        for tb in range(min(2, NTB)):
            xre_load(tb)

        for tb in range(NTB):
            gtb = tg * NTB + tb
            for dg in range(2):
                i = tb * 2 + dg
                par, w = tl_use([f"bzn_{tg}_{NCC - 1}", "bwl_0"])
                for cc in range(NCC):
                    E("tensor", lambda e, cc=cc, tb=tb, dg=dg, par=par, tg=tg: e.matmul(
                        tl[par][:], ztg[tg % 2][:, cc, 128 * tb:128 * (tb + 1)],
                        woT[:, cc, 512 * dg:512 * (dg + 1)],
                        start=(cc == 0), stop=(cc == NCC - 1)),
                      waits=w if cc == 0 else [], sem="stm",
                      tag=(f"bwom_{tg}_{i}" if cc == NCC - 1 else None))
                wev = [f"bwom_{tg}_{i}", "borev_1"]
                if i >= 2:
                    wev.append(OPT(f"bx1_{tg}_{i - 2}"))
                elif tg >= 1:
                    wev.append(OPT(f"bx1_{tg - 1}_{i + 6}"))
                E("vector", lambda e, par=par, dg=dg, i=i: e.tensor_add(
                    t1f[i % 2][:], tl[par][:], borep[:, 512 * dg:512 * (dg + 1)]),
                  waits=wev, sem="svv", tag=f"bwoe_{tg}_{i}")
                tl_done(par, f"bwoe_{tg}_{i}")
                xw = [f"bwoe_{tg}_{i}", f"bxl_{gtb}"]
                if tg >= 1:
                    xw.append(OPT(f"bod_{gtb - NTB}"))
                E("gpsimd", lambda e, tb=tb, dg=dg, i=i: e.tensor_add(
                    x1[:, tb, 512 * dg:512 * (dg + 1)], t1f[i % 2][:],
                    xre[:, tb % 2, 512 * dg:512 * (dg + 1)]),
                  waits=xw, sem="sgp", tag=f"bx1_{tg}_{i}")
            if debug_taps:
                E("sync", lambda e, tg=tg, tb=tb: e.dma_start(
                    out=x1tap_d.rearrange("(ti p) c -> p ti c", p=128)[:, tg * NTB + tb, :],
                    in_=x1[:, tb, :]),
                  waits=[f"bx1_{tg}_{2 * tb}", f"bx1_{tg}_{2 * tb + 1}"],
                  sem="sxt", dma=True, tag=f"bx1t_{gtb}")
            if tb + 2 < NTB:
                xre_load(tb + 2)
            nb = gtb % 2
            bw = [f"bx1_{tg}_{2 * tb}", f"bx1_{tg}_{2 * tb + 1}"]
            if gtb >= 1:
                bw.append(f"bnrm2_{gtb - 1}")
            E("vector", lambda e, tb=tb: e.bn_stats(bn12b[:, 0:6], x1[:, tb, 0:512]),
              waits=bw, sem="svv", tag=f"bbs1_{gtb}")
            E("vector", lambda e, tb=tb: e.bn_stats(bn12b[:, 6:12], x1[:, tb, 512:1024]),
              sem="svv", tag=f"bbs2_{gtb}")
            E("vector", lambda e: e.bn_aggr(bnagb[:], bn12b[:]),
              waits=[f"bbs1_{gtb}", f"bbs2_{gtb}"], sem="svv", tag=f"bbn2_{gtb}")
            sw = [f"bbn2_{gtb}"]
            if gtb >= 2:
                sw.append(OPT(f"bu2n_{gtb - 2}"))
            E("scalar", lambda e, nb=nb: e.activation(nrm2[nb][:, 1:2], bnagb[:, 1:2],
                                                      AF.Ln, bias=cst[:, 0:1]),
              waits=sw, sem="ssc", tag=f"bnl_{gtb}")
            E("scalar", lambda e, nb=nb: e.activation(nrm2[nb][:, 1:2], nrm2[nb][:, 1:2],
                                                      AF.Exp, scale=cst[:, 2:3]),
              waits=[f"bnl_{gtb}"], sem="ssc", tag=f"bne_{gtb}")
            E("scalar", lambda e, nb=nb: e.copy(nrm2[nb][:, 0:1], bnagb[:, 0:1]),
              waits=[f"bne_{gtb}"], sem="ssc", tag=f"bnrm2_{gtb}")
            uw = [f"bnrm2_{gtb}"]
            if gtb >= 2:
                uw.append(OPT(f"bu2tr_{gtb - 2}_1"))
            E("vector", lambda e, tb=tb, nb=nb, gtb=gtb: e.tensor_scalar(
                u2tm[gtb % 2][:], x1[:, tb, :], nrm2[nb][:, 0:1], nrm2[nb][:, 1:2],
                ALU.subtract, ALU.mult),
              waits=uw, sem="svv", tag=f"bu2n_{gtb}")
            for gg2 in range(2):
                par, w = tl_use([f"bu2n_{gtb}"])
                for blk in range(4):
                    E("tensor", lambda e, gtb=gtb, gg2=gg2, blk=blk, par=par: e.transpose(
                        tl[par][:, 128 * blk:128 * (blk + 1)],
                        u2tm[gtb % 2][:, 512 * gg2 + 128 * blk: 512 * gg2 + 128 * (blk + 1)],
                        identf[:]),
                      waits=w if blk == 0 else [], sem="stm",
                      tag=(f"bu2tr_{gtb}_{gg2}" if blk == 3 else None))
                ew = [f"bu2tr_{gtb}_{gg2}"]
                if tg >= 1:
                    ew.append(OPT(f"bt2_{tg - 1}_7"))
                else:
                    ew.append("bnd_init")
                E("vector", lambda e, tb=tb, gg2=gg2, par=par: e.tensor_copy(
                    u2fm[:, 4 * gg2:4 * (gg2 + 1), 128 * tb:128 * (tb + 1)],
                    tl[par][:].rearrange("p (c k) -> p c k", k=128)),
                  waits=ew, sem="svv", tag=f"bu2e_{gtb}_{gg2}")
                tl_done(par, f"bu2e_{gtb}_{gg2}")
            xrw = [f"bu2e_{gtb}_0", f"bu2e_{gtb}_1", "mfm_7_15"]
            if tg >= 1:
                xrw.append(OPT(f"bwvm_{tg - 1}_7"))
            if tb == 0:
                xrw.append("bnd_init" if tg == 0 else f"bbndc_{tg - 1}")
            lo = 128 * tb
            if tb == 0:
                E("vector", lambda e: e.tensor_sub(tmpv[:, :, 1:128],
                                                   u2fm[:, :, 1:128], u2fm[:, :, 0:127]),
                  waits=xrw, sem="svv")
                E("vector", lambda e: e.tensor_sub(tmpv[:, :, 0:1], u2fm[:, :, 0:1], bcol[:]),
                  sem="svv", tag=f"bxrm1_{gtb}")
            else:
                E("vector", lambda e, lo=lo: e.tensor_sub(
                    tmpv[:], u2fm[:, :, lo:lo + 128], u2fm[:, :, lo - 1:lo + 127]),
                  waits=xrw, sem="svv", tag=f"bxrm1_{gtb}")
            E("vector", lambda e: e.tensor_mul(tmpv[:], tmpv[:], mufm[:]),
              waits=[f"bxrm1_{gtb}"], sem="svv", tag=f"bxrm2_{gtb}")
            if tb == 0:
                E("vector", lambda e, lo=lo: e.tensor_add(
                    xr[:, :, lo + 1:lo + 128], tmpv[:, :, 1:128], u2fm[:, :, 0:127]),
                  waits=[f"bxrm2_{gtb}"], sem="svv")
                E("vector", lambda e, lo=lo: e.tensor_add(
                    xr[:, :, lo:lo + 1], tmpv[:, :, 0:1], bcol[:]),
                  sem="svv", tag=f"bxr_{gtb}")
            else:
                E("vector", lambda e, lo=lo: e.tensor_add(
                    xr[:, :, lo:lo + 128], tmpv[:], u2fm[:, :, lo - 1:lo + 127]),
                  waits=[f"bxrm2_{gtb}"], sem="svv", tag=f"bxr_{gtb}")
            xkw = [f"bu2e_{gtb}_0", f"bu2e_{gtb}_1", "mfm_7_15"]
            if tg >= 1:
                xkw.append(OPT(f"bwkm_{tg - 1}_7"))
            if tb == 0:
                xkw.append("bnd_init" if tg == 0 else f"bbndc_{tg - 1}")
            if tb == 0:
                E("gpsimd", lambda e: e.tensor_sub(tmpg[:, :, 1:128],
                                                   u2fm[:, :, 1:128], u2fm[:, :, 0:127]),
                  waits=xkw, sem="sgp")
                E("gpsimd", lambda e: e.tensor_sub(tmpg[:, :, 0:1], u2fm[:, :, 0:1], bcol[:]),
                  sem="sgp", tag=f"bxkm1_{gtb}")
            else:
                E("gpsimd", lambda e, lo=lo: e.tensor_sub(
                    tmpg[:], u2fm[:, :, lo:lo + 128], u2fm[:, :, lo - 1:lo + 127]),
                  waits=xkw, sem="sgp", tag=f"bxkm1_{gtb}")
            E("gpsimd", lambda e: e.tensor_mul(tmpg[:], tmpg[:], mkfm[:]),
              waits=[f"bxkm1_{gtb}"], sem="sgp", tag=f"bxkm2_{gtb}")
            if tb == 0:
                E("gpsimd", lambda e, lo=lo: e.tensor_add(
                    xk[:, :, lo + 1:lo + 128], tmpg[:, :, 1:128], u2fm[:, :, 0:127]),
                  waits=[f"bxkm2_{gtb}"], sem="sgp")
                E("gpsimd", lambda e, lo=lo: e.tensor_add(
                    xk[:, :, lo:lo + 1], tmpg[:, :, 0:1], bcol[:]),
                  sem="sgp", tag=f"bxk_{gtb}")
            else:
                E("gpsimd", lambda e, lo=lo: e.tensor_add(
                    xk[:, :, lo:lo + 128], tmpg[:], u2fm[:, :, lo - 1:lo + 127]),
                  waits=[f"bxkm2_{gtb}"], sem="sgp", tag=f"bxk_{gtb}")
        E("vector", lambda e: e.tensor_copy(bcol[:], u2fm[:, :, TG - 1:TG]),
          waits=[f"bxr_{tg * NTB + NTB - 1}", f"bxk_{tg * NTB + NTB - 1}"],
          sem="svv", tag=f"bbndc_{tg}")

        for tb in range(NTB):
            gtb = tg * NTB + tb
            for dg in range(2):
                i = tb * 2 + dg
                par, w = tl_use([f"bxr_{gtb}", "bwl_1"])
                if tg >= 1:
                    w.append(OPT(f"bt2_{tg - 1}_{i}"))
                if i == 0:
                    w.append(f"bbndc_{tg}")
                for cc in range(NCC):
                    E("tensor", lambda e, cc=cc, tb=tb, dg=dg, par=par: e.matmul(
                        tl[par][:], xr[:, cc, 128 * tb:128 * (tb + 1)],
                        wrT[:, cc, 512 * dg:512 * (dg + 1)],
                        start=(cc == 0), stop=(cc == NCC - 1)),
                      waits=w if cc == 0 else [], sem="stm",
                      tag=(f"bwrm_{tg}_{i}" if cc == NCC - 1 else None))
                E("scalar", lambda e, i=i, par=par: e.activation(
                    th[:, i, :], tl[par][:], AF.Tanh, scale=cst[:, 3:4]),
                  waits=[f"bwrm_{tg}_{i}"], sem="ssc", tag=f"bth_{tg}_{i}")
                tl_done(par, f"bth_{tg}_{i}")
        for dc in range(NCC):
            par, w = tl_use([f"bxk_{tg * NTB + NTB - 1}", "bwl_2"])
            if dc == 0:
                w.append(f"bwrm_{tg}_7")
            for cc in range(NCC):
                E("tensor", lambda e, cc=cc, dc=dc, par=par: e.matmul(
                    tl[par][:], wkT[:, cc, 128 * dc:128 * (dc + 1)], xk[:, cc, :],
                    start=(cc == 0), stop=(cc == NCC - 1)),
                  waits=w if cc == 0 else [], sem="stm",
                  tag=(f"bwkm_{tg}_{dc}" if cc == NCC - 1 else None))
            kw = [f"bwkm_{tg}_{dc}"]
            if dc >= 1:
                kw.append(f"bksq_{tg}_{dc - 1}")
            E("vector", lambda e, dc=dc, par=par: e.tensor_scalar_max(
                krelu[0][:], tl[par][:], 0.0),
              waits=kw, sem="svv", tag=f"bkrl_{tg}_{dc}")
            tl_done(par, f"bkrl_{tg}_{dc}")
            kqw = [f"bkrl_{tg}_{dc}"]
            if dc == 0:
                kqw.append(f"bwrm_{tg}_7")
            E("vector", lambda e, dc=dc: e.tensor_mul(
                kact[:, dc, :], krelu[0][:], krelu[0][:]),
              waits=kqw, sem="svv", tag=f"bksq_{tg}_{dc}")
        for tb in range(NTB):
            gtb = tg * NTB + tb
            for dg in range(2):
                i = tb * 2 + dg
                par, w = tl_use([f"bksq_{tg}_{NCC - 1}", "bwl_3"])
                for dk in range(NCC):
                    E("tensor", lambda e, dk=dk, tb=tb, dg=dg, par=par: e.matmul(
                        tl[par][:], kact[:, dk, 128 * tb:128 * (tb + 1)],
                        wvT[:, dk, 512 * dg:512 * (dg + 1)],
                        start=(dk == 0), stop=(dk == NCC - 1)),
                      waits=w if dk == 0 else [], sem="stm",
                      tag=(f"bwvm_{tg}_{i}" if dk == NCC - 1 else None))
                vw = [f"bwvm_{tg}_{i}", f"bth_{tg}_{i}"]
                if i >= 2:
                    vw.append(f"bfin_{tg * NTB + (i - 2) // 2}_{(i - 2) % 2}")
                elif tg >= 1:
                    vw.append(OPT(f"bfin_{(tg - 1) * NTB + (i + 6) // 2}_{(i + 6) % 2}"))
                E("vector", lambda e, par=par, i=i: e.tensor_scalar(
                    t2f[i % 2][:], th[:, i, :], cst[:, 3:4], cst[:, 3:4],
                    ALU.mult, ALU.add),
                  waits=vw, sem="svv", tag=f"bsg_{tg}_{i}")
                E("vector", lambda e, par=par, i=i: e.tensor_mul(
                    t2f[i % 2][:], t2f[i % 2][:], tl[par][:]),
                  waits=[f"bsg_{tg}_{i}"], sem="svv", tag=f"bt2_{tg}_{i}")
                tl_done(par, f"bt2_{tg}_{i}")
                E("gpsimd", lambda e, tb=tb, dg=dg, i=i: e.tensor_add(
                    x1[:, tb, 512 * dg:512 * (dg + 1)], t2f[i % 2][:],
                    x1[:, tb, 512 * dg:512 * (dg + 1)]),
                  waits=[f"bt2_{tg}_{i}", f"bx1_{tg}_{i}"] +
                        ([OPT(f"bx1t_{gtb}")] if debug_taps else []),
                  sem="sgp", tag=f"bfin_{gtb}_{dg}")
            E("sync", lambda e, tg=tg, tb=tb: e.dma_start(
                out=out_t[:, tg * NTB + tb, :], in_=x1[:, tb, :]),
              waits=[f"bfin_{gtb}_0", f"bfin_{gtb}_1"], sem="sod", dma=True,
              tag=f"bod_{gtb}")

    mchain(0)
    if NU > 1:
        mchain(1)
    pend_tail = []
    tq = None
    for u in range(NU):
        l2 = P.capture(unit_p2, u)
        if u == NU - 1:
            l2 += P.capture(unit_carry, u)
        take = len(pend_tail) if len(pend_tail) <= 16 else max(10, len(pend_tail) // 10)
        half = pend_tail[:take]
        rest = pend_tail[take:]
        P.merge(l2, half)
        pend_tail = rest
        if tq is not None:
            pend_tail = pend_tail + tq
            tq = None
        tg, h = divmod(u, H)
        if h == H - 1:
            tq = P.capture(tail, tg)
    emit_bwl(4 * NCC)
    P.merge(pend_tail + (tq or []))

    E("sync", lambda e: e.nop(), waits=[("sod", 16 * NTG * NTB)], sem="sfin", tag="fin")

    with nc.Block() as block:
        P.run(block)
    esB.close()
    es.close()
    return nc


def kernel(L0_attn_x, L0_attn_Wqkv, L0_attn_bqkv, L0_attn_Wo, L0_attn_bo,
           L0_attn_wq, L0_attn_wk, L0_ffn_W_r, L0_ffn_W_k, L0_ffn_W_v,
           L0_ffn_mu_r, L0_ffn_mu_k):
    x = np.ascontiguousarray(np.asarray(L0_attn_x, dtype=np.float32))
    B, T, _ = x.shape
    nc = build_nc(T=T)
    shared = {
        "wqkv": np.ascontiguousarray(L0_attn_Wqkv, dtype=np.float32),
        "bqkv": np.ascontiguousarray(L0_attn_bqkv, dtype=np.float32),
        "wo": np.ascontiguousarray(L0_attn_Wo, dtype=np.float32),
        "bo": np.ascontiguousarray(L0_attn_bo, dtype=np.float32),
        "wqg": np.ascontiguousarray(L0_attn_wq, dtype=np.float32),
        "wkg": np.ascontiguousarray(L0_attn_wk, dtype=np.float32),
        "wr": np.ascontiguousarray(L0_ffn_W_r, dtype=np.float32),
        "wkf": np.ascontiguousarray(L0_ffn_W_k, dtype=np.float32),
        "wv": np.ascontiguousarray(L0_ffn_W_v, dtype=np.float32),
        "mur": np.ascontiguousarray(L0_ffn_mu_r, dtype=np.float32),
        "muk": np.ascontiguousarray(L0_ffn_mu_k, dtype=np.float32),
    }
    in_maps = [dict(shared, x=np.ascontiguousarray(x[b])) for b in range(B)]
    res = run_bass_kernel_spmd(nc, in_maps, core_ids=list(range(B)))
    out = np.stack([np.asarray(res.results[b]["out"], dtype=np.float32)
                    for b in range(B)], axis=0)
    return out


# version 42
# speedup vs baseline: 1.1851x; 1.1851x over previous
import contextlib
import numpy as np

import concourse.bass as bass
import concourse.mybir as mybir
from concourse.bass_utils import run_bass_kernel_spmd

F32 = mybir.dt.float32
BF16 = mybir.dt.bfloat16
AX = mybir.AxisListType.X
AF = mybir.ActivationFunctionType
ALU = mybir.AluOpType

D = 1024
H = 16
DH = 64
NCC = 8
NP = 8


def OPT(tag):
    return ("OPT", tag)


class Prog:
    ENGINES = ("tensor", "vector", "scalar", "gpsimd", "sync")

    def __init__(self, nc, es):
        self.nc = nc
        self.es = es
        self.items = []
        self.sems = {}

    def _sem(self, name):
        if name not in self.sems:
            self.sems[name] = self.es.enter_context(self.nc.semaphore(name))
        return self.sems[name]

    def emit(self, eng, build, waits=(), sem=None, dma=False, tag=None):
        assert eng in self.ENGINES
        if sem is not None:
            self._sem(sem)
        self.items.append([eng, build, list(waits), sem, dma, tag])

    def capture(self, fn, *a, **kw):
        old = self.items
        self.items = []
        fn(*a, **kw)
        out = self.items
        self.items = old
        return out

    def merge(self, *lists):
        lists = [l for l in lists if l]
        idx = [0] * len(lists)
        while True:
            best = -1
            bestf = 2.0
            for i, l in enumerate(lists):
                if idx[i] < len(l):
                    f = idx[i] / len(l)
                    if f < bestf:
                        bestf = f
                        best = i
            if best < 0:
                break
            self.items.append(lists[best][idx[best]])
            idx[best] += 1

    def run(self, block):
        counts = {s: 0 for s in self.sems}
        events = {}
        resolved = []
        for eng, build, waits, sem, dma, tag in self.items:
            rw = []
            for w in waits:
                if isinstance(w, tuple) and w[0] == "OPT":
                    if w[1] not in events:
                        continue
                    w = w[1]
                if isinstance(w, tuple):
                    rw.append((w[0], w[1], None))
                else:
                    if w not in events:
                        raise KeyError(f"wait on unknown tag {w!r}")
                    s, v, peng, pdma = events[w]
                    if pdma:
                        rw.append((s, counts[s], None))
                    else:
                        rw.append((s, v, peng))
            val = None
            if sem is not None:
                counts[sem] += 16 if dma else 1
                val = counts[sem]
            if tag is not None:
                assert sem is not None, f"tag {tag} needs a sem"
                assert tag not in events, f"duplicate tag {tag}"
                events[tag] = (sem, val, eng, dma)
            resolved.append((eng, build, rw, sem, dma))
        self.final_counts = dict(counts)
        self.events = events

        def make(name):
            def f(e):
                last = {}
                for eng, build, rw, sem, dma in resolved:
                    if eng != name:
                        continue
                    byname = {}
                    for s, v, peng in rw:
                        if v > byname.get(s, -1):
                            byname[s] = v
                    for s, v in byname.items():
                        if last.get(s, -1) >= v:
                            continue
                        e.wait_ge(self.sems[s], v)
                        last[s] = v
                    inst = build(e)
                    if sem is not None:
                        inst.then_inc(self.sems[sem], 16 if dma else 1)
            return f

        block.tensor(make("tensor"))
        block.vector(make("vector"))
        block.scalar(make("scalar"))
        block.gpsimd(make("gpsimd"))
        block.sync(make("sync"))


def build_nc(T=4096, debug_taps=False):
    TG = 512
    NTG = T // TG
    ST = min(2048, T)
    NST = T // ST
    NTS = ST // 512
    NSC = T // 128
    NQT = TG // 128
    NTB = TG // 128
    NTI = ST // 128
    NBLK = T // 128
    NU = NTG * H

    nc = bass.Bass("TRN2", target_bir_lowering=False)

    x_h = nc.dram_tensor("x", [T, D], F32, kind="ExternalInput").ap()
    wqkv_h = nc.dram_tensor("wqkv", [3 * D, D], F32, kind="ExternalInput").ap()
    bqkv_h = nc.dram_tensor("bqkv", [3 * D], F32, kind="ExternalInput").ap()
    wo_h = nc.dram_tensor("wo", [D, D], F32, kind="ExternalInput").ap()
    bo_h = nc.dram_tensor("bo", [D], F32, kind="ExternalInput").ap()
    wq_h = nc.dram_tensor("wqg", [DH], F32, kind="ExternalInput").ap()
    wk_h = nc.dram_tensor("wkg", [DH], F32, kind="ExternalInput").ap()
    wr_h = nc.dram_tensor("wr", [D, D], F32, kind="ExternalInput").ap()
    wkf_h = nc.dram_tensor("wkf", [D, D], F32, kind="ExternalInput").ap()
    wv_h = nc.dram_tensor("wv", [D, D], F32, kind="ExternalInput").ap()
    mur_h = nc.dram_tensor("mur", [D], F32, kind="ExternalInput").ap()
    muk_h = nc.dram_tensor("muk", [D], F32, kind="ExternalInput").ap()
    out_h = nc.dram_tensor("out", [T, D], F32, kind="ExternalOutput").ap()

    ktap = "ExternalOutput" if debug_taps else "Internal"
    wTq_d = nc.dram_tensor("wTqkv", [NCC, 128, 3 * D], BF16, kind="Internal").ap()
    wTo_d = nc.dram_tensor("wTo", [NCC, 128, D], BF16, kind="Internal").ap()
    wTr_d = nc.dram_tensor("wTr", [NCC, 128, D], BF16, kind="Internal").ap()
    wTk_d = nc.dram_tensor("wTk", [NCC, 128, D], BF16, kind="Internal").ap()
    wTv_d = nc.dram_tensor("wTv", [NCC, 128, D], BF16, kind="Internal").ap()
    qsp_d = nc.dram_tensor("qsp", [NP, 128, T], BF16, kind=ktap).ap()
    ltap_d = nc.dram_tensor("ltap", [NTG, 16, TG], BF16, kind=ktap).ap()
    x1tap_d = nc.dram_tensor("x1tap", [T, D], F32, kind=ktap).ap()
    ztap_d = nc.dram_tensor("ztap", [NTG, 65, 4, TG], BF16, kind=ktap).ap()
    ksp_d = nc.dram_tensor("ksp", [NP, 128, T], BF16, kind=ktap).ap()
    vhat_d = nc.dram_tensor("vhat", [H, 128, NBLK * 65], BF16, kind=ktap).ap()
    mq_d = nc.dram_tensor("mq", [H, T], BF16, kind=ktap).ap()
    wq2_d = nc.dram_tensor("wq2", [DH], F32, kind="Internal").ap()
    wk2_d = nc.dram_tensor("wk2", [DH], F32, kind="Internal").ap()

    x_t = x_h.rearrange("(ti p) c -> p ti c", p=128)
    out_t = out_h.rearrange("(ti p) c -> p ti c", p=128)

    es = contextlib.ExitStack()
    P = Prog(nc, es)
    E = P.emit

    def sbt(name, shape, dt=BF16, stk=None):
        return (stk or es).enter_context(nc.sbuf_tensor(name, shape, dt))

    def pst(name, shape, dt=F32, stk=None):
        return (stk or es).enter_context(nc.psum_tensor(name, shape, dt))

    ident = sbt("ident", [128, 128], BF16)
    identf = sbt("identf", [128, 128], F32)
    cst = sbt("cst", [128, 8], F32)
    bq_sb = sbt("bq", [128, 24], F32)
    mu_sb = sbt("mu", [128, 16], F32)
    mufm = sbt("mufm", [128, NCC, 128])
    mkfm = sbt("mkfm", [128, NCC, 128])
    ones128 = sbt("ones128", [128, 128])
    borep = sbt("borep", [128, D])
    E2 = sbt("E2", [96, D])
    alpha8 = sbt("alpha8", [8, 1], F32)
    kmx8 = sbt("kmx8", [8, 1], F32)
    arow = sbt("arow", [1, 1], F32)
    alpham = sbt("alpham", [1, 1], F32)
    alphab = sbt("alphab", [1, 1])
    a16 = sbt("a16", [128, 1], F32)
    wq2row = sbt("wq2row", [1, DH], F32)
    wk2row = sbt("wk2row", [1, DH], F32)

    esA = contextlib.ExitStack()
    borow = sbt("borow", [1, D], F32, stk=esA)
    bvrow_bf = sbt("bvrowbf", [1, D], stk=esA)
    ones_row = sbt("onesrow", [1, 128], stk=esA)
    ones1f = sbt("ones1f", [1, 128], F32, stk=esA)
    blkq = [sbt(f"blkq_{t}", [128, 40], stk=esA) for t in range(NTS)]
    blkk = [sbt(f"blkk_{t}", [128, 40], stk=esA) for t in range(NTS)]
    msr = [sbt("msr0", [8, 512], stk=esA)]
    E8q = [sbt(f"e8q_{t}", [96, 128], stk=esA) for t in range(NTS)]
    E8k = [sbt(f"e8k_{t}", [96, 128], stk=esA) for t in range(NTS)]
    wqrow = sbt("wqrow", [1, DH], stk=esA)
    wkrow = sbt("wkrow", [1, DH], stk=esA)
    wnat = [sbt(f"wnat{i}", [128, D], stk=esA) for i in range(2)]
    wTsb = [sbt(f"wTsb{i}", [128, D], stk=esA) for i in range(2)]
    xst = [sbt(f"xst{i}", [128, 2, D], F32, stk=esA) for i in range(2)]
    u1 = [sbt(f"u1_{i}", [128, NCC, ST], stk=esA) for i in range(2)]
    u1tm = [sbt(f"u1tm{i}", [128, D], stk=esA) for i in range(2)]
    bn12 = sbt("bn12", [128, 12], F32, stk=esA)
    bnag = sbt("bnag", [128, 2], F32, stk=esA)
    nrm = [sbt(f"nrm{i}", [128, 2], F32, stk=esA) for i in range(2)]
    wpair = [sbt(f"wpair{i}", [128, 3, NCC, 128], stk=esA) for i in range(3)]
    qraw = [sbt(f"qraw{i}", [128, ST], F32, stk=esA) for i in range(4)]
    qsq = [sbt(f"qsq{i}", [128, ST], stk=esA) for i in range(4)]
    rrs = sbt("rrs", [96, 512], stk=esA)
    spl = [sbt(f"spl{i}", [128, ST], stk=esA) for i in range(4)]
    vstg = [sbt(f"vstg{i}", [128, 4, 2, 65], stk=esA) for i in range(2)]

    pq = pst("pq", [128, 1024], stk=esA)
    prms = pst("prms", [40, 512], stk=esA)
    pexp = [pst(f"pexp{i}", [128, 512], stk=esA) for i in range(2)]
    pv = pst("pv", [128, 512], stk=esA)
    pu1T = pst("pu1T", [128, 1024], BF16, stk=esA)
    pwtp = pst("pwtp", [128, 1024], BF16, stk=esA)

    E("gpsimd", lambda e: e.memset(ident[:], 0.0), sem="sgp", tag="idz")
    E("gpsimd", lambda e: e.affine_select(
        out=ident[:], in_=ident[:], pattern=[[-1, 128]],
        compare_op=ALU.not_equal, fill=1.0, base=0, channel_multiplier=1),
      waits=["idz"], sem="sgp", tag="idsel")
    E("gpsimd", lambda e: e.memset(identf[:], 0.0), sem="sgp", tag="idzf")
    E("gpsimd", lambda e: e.affine_select(
        out=identf[:], in_=identf[:], pattern=[[-1, 128]],
        compare_op=ALU.not_equal, fill=1.0, base=0, channel_multiplier=1),
      waits=["idzf"], sem="sgp", tag="idself")
    for col, v in [(0, 1e-5), (1, 1e-6), (2, -0.5), (3, 0.5), (4, 1.0),
                   (5, 1.0 / 64.0), (6, -1.0)]:
        E("gpsimd", lambda e, c=col, vv=v: e.memset(cst[:, c:c + 1], vv))
    E("gpsimd", lambda e: e.memset(ones_row[:], 1.0))
    E("gpsimd", lambda e: e.memset(ones1f[:], 1.0))
    E("gpsimd", lambda e: e.memset(ones128[:], 1.0), sem="sgp", tag="ones128d")
    E("gpsimd", lambda e: e.memset(E2[:], 0.0), sem="sgp", tag="e2z")
    for h in range(16):
        E("sync", lambda e, h=h: e.dma_start(
            out=E2[h:h + 1, DH * h:DH * (h + 1)], in_=ones128[0:1, 0:DH]),
          waits=["e2z", "ones128d"], sem="sld", dma=True, tag=f"e2r_{h}")
    for t in range(NTS):
        for nm, blkx in (("q", blkq), ("k", blkk)):
            E("gpsimd", lambda e, t=t, blkx=blkx: e.memset(blkx[t][:], 0.0),
              sem="sgp", tag=f"bkz{nm}_{t}")
            E("gpsimd", lambda e, t=t, blkx=blkx: e.memset(blkx[t][0:64, t:t + 1], 1.0),
              waits=[f"bkz{nm}_{t}"], sem="sgp", tag=f"bka{nm}_{t}")
            E("gpsimd", lambda e, t=t, blkx=blkx: e.memset(
                blkx[t][64:128, NTS + t:NTS + t + 1], 1.0),
              waits=[f"bkz{nm}_{t}"], sem="sgp", tag=f"bkb{nm}_{t}")
        E("gpsimd", lambda e, t=t: e.memset(E8q[t][:], 0.0), sem="sgp", tag=f"e8qz_{t}")
        E("gpsimd", lambda e, t=t: e.memset(E8k[t][:], 0.0), sem="sgp", tag=f"e8kz_{t}")
    E("gpsimd", lambda e: e.memset(alpha8[:], 0.0), sem="sgp", tag="alz")
    for i in range(2):
        E("gpsimd", lambda e, i=i: e.memset(vstg[i][:, :, :, 64:65], 1.0),
          sem="sgp", tag=f"vones{i}")
    E("gpsimd", lambda e: e.memset(rrs[0:96, :], 0.0), sem="sgp", tag="rrsz")
    E("gpsimd", lambda e: e.memset(cst[:, 7:8], 0.0), sem="sgp", tag="consts")

    E("sync", lambda e: e.dma_start(out=bq_sb[:], in_=bqkv_h.rearrange("(c p) -> p c", p=128), allow_slow_non_contiguous=True),
      sem="sld", dma=True, tag="ld_bq")
    E("sync", lambda e: e.dma_start(out=mu_sb[:, 0:8], in_=mur_h.rearrange("(c p) -> p c", p=128), allow_slow_non_contiguous=True),
      sem="sld", dma=True, tag="ld_mur")
    E("sync", lambda e: e.dma_start(out=mu_sb[:, 8:16], in_=muk_h.rearrange("(c p) -> p c", p=128), allow_slow_non_contiguous=True),
      sem="sld", dma=True, tag="ld_muk")
    E("sync", lambda e: e.dma_start(out=borow[:], in_=bo_h.rearrange("(o d) -> o d", o=1)),
      sem="sld", dma=True, tag="ld_bo")
    E("gpsimd", lambda e: e.dma_start(out=bvrow_bf[:], in_=bqkv_h[2 * D:3 * D].rearrange("(o d) -> o d", o=1)),
      sem="sgl", dma=True, tag="bvbf")
    E("gpsimd", lambda e: e.dma_start(out=wqrow[:], in_=wq_h.rearrange("(o d) -> o d", o=1)),
      sem="sgl", dma=True, tag="ld_wq")
    E("gpsimd", lambda e: e.dma_start(out=wkrow[:], in_=wk_h.rearrange("(o d) -> o d", o=1)),
      sem="sgl", dma=True, tag="ld_wk")
    E("vector", lambda e: e.tensor_mul(wq2row[:], wqrow[:], wqrow[:]),
      waits=["ld_wq"], sem="svv", tag="wq2v")
    E("vector", lambda e: e.tensor_mul(wk2row[:], wkrow[:], wkrow[:]),
      waits=["ld_wk"], sem="svv", tag="wk2v")
    E("sync", lambda e: e.dma_start(out=wq2_d.rearrange("(o d) -> o d", o=1), in_=wq2row[:]),
      waits=["wq2v"], sem="sld", dma=True, tag="wq2d")
    E("sync", lambda e: e.dma_start(out=wk2_d.rearrange("(o d) -> o d", o=1), in_=wk2row[:]),
      waits=["wk2v"], sem="sld", dma=True, tag="wk2d")
    for t in range(NTS):
        for nm, blkx, gsrc in (("q", blkq, wq2_d), ("k", blkk, wk2_d)):
            E("gpsimd", lambda e, t=t, blkx=blkx, gsrc=gsrc: e.dma_start(
                out=blkx[t][0:64, 32 + t:33 + t],
                in_=gsrc.rearrange("(p o) -> p o", o=1)),
              waits=["wq2d", "wk2d", f"bkz{nm}_{t}"], sem="sgl", dma=True)
            E("gpsimd", lambda e, t=t, blkx=blkx, gsrc=gsrc: e.dma_start(
                out=blkx[t][64:128, 32 + NTS + t:33 + NTS + t],
                in_=gsrc.rearrange("(p o) -> p o", o=1)),
              sem="sgl", dma=True, tag=f"blkw{nm}_{t}")
    for t in range(NTS):
        for (Ew, row, tgd) in [(E8q, wqrow, "q"), (E8k, wkrow, "k")]:
            E("sync", lambda e, Ew=Ew, t=t, row=row: e.dma_start(
                out=Ew[t][t:t + 1, 0:DH], in_=row[:]),
              waits=[f"ld_w{tgd}", "consts", f"e8{tgd}z_{t}"], sem="sld", dma=True)
            E("sync", lambda e, Ew=Ew, t=t, row=row: e.dma_start(
                out=Ew[t][NTS + t:NTS + t + 1, DH:128], in_=row[:]),
              sem="sld", dma=True, tag=f"ld_e8{tgd}_{t}")
    for cc in range(NCC):
        for (dst, col) in [(mufm, cc), (mkfm, 8 + cc)]:
            E("vector", lambda e, dst=dst, cc=cc, col=col: e.tensor_scalar_mul(
                dst[:, cc, :], ones128[:], mu_sb[:, col:col + 1]),
              waits=["ld_mur", "ld_muk", "consts", "ones128d"], sem="svv",
              tag=f"mfm_{cc}_{col}")
    for g in range(2):
        E("tensor", lambda e, g=g: e.matmul(
            pq[:, 512 * g:512 * (g + 1)], ones1f[:],
            borow[0:1, 512 * g:512 * (g + 1)], start=True, stop=True),
          waits=["ld_bo", "consts"], sem="stm", tag=f"bormm_{g}")
        E("vector", lambda e, g=g: e.tensor_copy(
            borep[:, 512 * g:512 * (g + 1)], pq[:, 512 * g:512 * (g + 1)]),
          waits=[f"bormm_{g}"], sem="svv", tag=f"borev_{g}")

    wsrc = [(wqkv_h, wTq_d, 3 * NCC), (wo_h, wTo_d, NCC), (wr_h, wTr_d, NCC),
            (wkf_h, wTk_d, NCC), (wv_h, wTv_d, NCC)]
    wrg = []
    for src, dstT, nrg in wsrc:
        for rg in range(nrg):
            wrg.append((src, dstT, rg))
    NW = len(wrg)
    NWQ = 3 * NCC

    def emit_w(g0, g1):
        for gi in range(g0, g1):
            src, dstT, rg = wrg[gi]
            b = gi % 2
            E("gpsimd", lambda e, src=src, rg=rg, b=b: e.dma_start(
                out=wnat[b][:], in_=src[128 * rg:128 * (rg + 1), :]),
              waits=(["consts"] if gi < 2 else [f"wtr_{gi - 2}"]),
              sem=f"swl{gi % 2}", dma=True, tag=f"wld_{gi}")
            tw = [f"wld_{gi}"] + ([f"wev_{gi - 1}"] if gi >= 1 else ["borev_1"])
            for blk in range(NCC):
                E("tensor", lambda e, b=b, blk=blk: e.transpose(
                    pwtp[:, 128 * blk:128 * (blk + 1)],
                    wnat[b][:, 128 * blk:128 * (blk + 1)], ident[:]),
                  waits=tw if blk == 0 else [],
                  sem="stm", tag=(f"wtr_{gi}" if blk == NCC - 1 else None))
            E("vector", lambda e, b=b: e.tensor_copy(wTsb[b][:], pwtp[:]),
              waits=[f"wtr_{gi}"] + ([f"wst_{gi - 2}"] if gi >= 2 else []),
              sem="svv", tag=f"wev_{gi}")
            E("sync", lambda e, dstT=dstT, rg=rg, b=b: e.dma_start(
                out=dstT[:, :, 128 * rg:128 * (rg + 1)].rearrange("c p k -> p c k"),
                in_=wTsb[b][:].rearrange("p (b k) -> p b k", k=128)),
              waits=[f"wev_{gi}"], sem=f"sws{gi % 2}", dma=True, tag=f"wst_{gi}")

    emit_w(0, NWQ)
    wrest = P.capture(emit_w, NWQ, NW)

    NGG = T // 512

    def emit_a1(st):
        for g in range(ST // 256):
            gg = st * (ST // 256) + g
            gb = gg % 2
            xw = [f"u1n_{gg - 2}_1"] if gg >= 2 else []
            E("sync", lambda e, st=st, g=g, gb=gb: e.dma_start(
                out=xst[gb][:], in_=x_t[:, st * NTI + 2 * g: st * NTI + 2 * (g + 1), :]),
              waits=xw, sem=f"sxl{gg % 2}", dma=True, tag=f"xld_{gg}")
            for j in range(2):
                ti = 2 * g + j
                tj = 2 * gg + j
                nb = tj % 2
                ub = tj % 2
                E("vector", lambda e, gb=gb, j=j: e.bn_stats(bn12[:, 0:6], xst[gb][:, j, 0:512]),
                  waits=[f"xld_{gg}"] + ([f"nrmd_{tj - 1}"] if tj >= 1 else []),
                  sem="svv", tag=f"bns1_{tj}")
                E("vector", lambda e, gb=gb, j=j: e.bn_stats(bn12[:, 6:12], xst[gb][:, j, 512:1024]),
                  sem="svv", tag=f"bns2_{tj}")
                E("vector", lambda e: e.bn_aggr(bnag[:], bn12[:]),
                  waits=[f"bns1_{tj}", f"bns2_{tj}"], sem="svv", tag=f"bn_{tj}")
                E("scalar", lambda e, nb=nb: e.activation(nrm[nb][:, 1:2], bnag[:, 1:2],
                                                          AF.Ln, bias=cst[:, 0:1]),
                  waits=[f"bn_{tj}"] + ([OPT(f"u1n_{(tj - 2) // 2}_{(tj - 2) % 2}")] if tj >= 2 else []),
                  sem="ssc", tag=f"nrl_{tj}")
                E("scalar", lambda e, nb=nb: e.activation(nrm[nb][:, 1:2], nrm[nb][:, 1:2],
                                                          AF.Exp, scale=cst[:, 2:3]),
                  waits=[f"nrl_{tj}"], sem="ssc", tag=f"nre_{tj}")
                E("scalar", lambda e, nb=nb: e.copy(nrm[nb][:, 0:1], bnag[:, 0:1]),
                  waits=[f"nre_{tj}"], sem="ssc", tag=f"nrmd_{tj}")
                u1w = [f"nrmd_{tj}"]
                if tj >= 2:
                    u1w.append(f"u1tr_{tj - 2}")
                E("vector", lambda e, gb=gb, j=j, nb=nb, ub=ub: e.tensor_scalar(
                    u1tm[ub][:], xst[gb][:, j, :], nrm[nb][:, 0:1], nrm[nb][:, 1:2],
                    ALU.subtract, ALU.mult),
                  waits=u1w, sem="svv", tag=f"u1n_{gg}_{j}")
                trw = [f"u1n_{gg}_{j}"]
                if tj >= 1:
                    trw.append(f"u1ev_{tj - 1}")
                if st >= 2 and ti == 0:
                    trw.append(f"a2v_{(st - 1) * NP - 1}_{NTS - 1}")
                for blk in range(NCC):
                    E("tensor", lambda e, ub=ub, blk=blk: e.transpose(
                        pu1T[:, 128 * blk:128 * (blk + 1)],
                        u1tm[ub][:, 128 * blk:128 * (blk + 1)], ident[:]),
                      waits=trw if blk == 0 else [],
                      sem="stm", tag=(f"u1tr_{tj}" if blk == NCC - 1 else None))
                E("vector", lambda e, ti=ti, st=st: e.tensor_copy(
                    u1[st % 2][:, :, 128 * ti:128 * (ti + 1)],
                    pu1T[:].rearrange("p (c k) -> p c k", k=128)),
                  waits=[f"u1tr_{tj}"], sem="svv", tag=f"u1ev_{tj}")
    def emit_wpl(pi):
        p = pi % NP
        pb = pi % 3
        for part3 in range(3):
            ww = [f"wst_{NWQ - 1}", f"wst_{NWQ - 2}"]
            if pi >= 3:
                ww.append(f"a2v_{pi - 3}_{NTS - 1}")
            eng = "sync" if part3 == 0 else "scalar"
            sem = f"swp{pi % 3}" if part3 == 0 else f"swq{pi % 3}"
            E(eng, lambda e, p=p, part3=part3, pb=pb: e.dma_start(
                out=wpair[pb][:, part3, :, :],
                in_=wTq_d[:, :, D * part3 + 128 * p: D * part3 + 128 * (p + 1)].rearrange(
                    "c p k -> p c k")),
              waits=ww, sem=sem, dma=True, tag=f"wpl_{pi}_{part3}")

    def emit_pair_mms(st, p):
        u1tag = f"u1ev_{(st + 1) * NTI - 1}"
        if True:
            pi = st * NP + p
            pb = pi % 3
            for part in range(2):
                sn = "q" if part == 0 else "k"
                bci = p if part == 0 else NP + p
                ps_seq = pi * 2 + part
                for t in range(NTS):
                    ph = (ps_seq * NTS + t) % 2
                    qs = ps_seq * NTS + t
                    mmw = [f"wpl_{pi}_{part}", u1tag]
                    if qs < 2:
                        mmw.append("borev_1")
                    if qs >= 2:
                        pq2 = qs - 2
                        pp, tt2 = divmod(pq2, NTS)
                        ppi, ppart = divmod(pp, 2)
                        mmw.append(f"a2sq_{ppi}_{'q' if ppart == 0 else 'k'}_{tt2}")
                    for cc in range(NCC):
                        E("tensor", lambda e, pb=pb, part=part, cc=cc, t=t, ph=ph, st=st: e.matmul(
                            pq[:, 512 * ph:512 * (ph + 1)], wpair[pb][:, part, cc, :],
                            u1[st % 2][:, cc, 512 * t:512 * (t + 1)],
                            start=(cc == 0), stop=(cc == NCC - 1)),
                          waits=mmw if cc == 0 else [],
                          sem="stm", tag=(f"a2mm_{pi}_{sn}_{t}" if cc == NCC - 1 else None))
                    bsel = 2 * (pi % 2) + part
                    E("vector", lambda e, bsel=bsel, t=t, ph=ph, bci=bci: e.tensor_scalar(
                        qraw[bsel][:, 512 * t:512 * (t + 1)], pq[:, 512 * ph:512 * (ph + 1)],
                        bq_sb[:, bci:bci + 1], None, ALU.add),
                      waits=[f"a2mm_{pi}_{sn}_{t}"] +
                            ([OPT(f"a2n_{pi - 2}_{sn}_{t}")] if pi >= 2 else []),
                      sem="svv", tag=f"a2raw_{pi}_{sn}_{t}")
                    E("vector", lambda e, bsel=bsel, t=t: e.tensor_mul(
                        qsq[bsel][:, 512 * t:512 * (t + 1)],
                        qraw[bsel][:, 512 * t:512 * (t + 1)],
                        qraw[bsel][:, 512 * t:512 * (t + 1)]),
                      waits=[f"a2raw_{pi}_{sn}_{t}"] +
                            ([OPT(f"a2rmm_{pi - 2}_{sn}")] if (t == 0 and pi >= 2) else []),
                      sem="svv", tag=f"a2sq_{pi}_{sn}_{t}")

            for tbg in range(NTS):
                vs = pi * NTS + tbg
                vb = vs % 2
                vw = [f"wpl_{pi}_2", u1tag, "bvbf"]
                if vs >= 1:
                    pv2 = vs - 1
                    vw.append(f"a2vev_{pv2 // NTS}_{pv2 % NTS}")
                for tb in range(4):
                    tt = 4 * tbg + tb
                    for cc in range(NCC):
                        E("tensor", lambda e, pb=pb, cc=cc, tt=tt, tb=tb, st=st: e.matmul(
                            pv[:, 128 * tb:128 * (tb + 1)],
                            u1[st % 2][:, cc, 128 * tt:128 * (tt + 1)], wpair[pb][:, 2, cc, :],
                            start=(cc == 0), stop=False),
                          waits=vw if (tb == 0 and cc == 0) else [],
                          sem="stm")
                    E("tensor", lambda e, p=p, tb=tb: e.matmul(
                        pv[:, 128 * tb:128 * (tb + 1)],
                        ones_row[:], bvrow_bf[:, 128 * p:128 * (p + 1)],
                        start=False, stop=True),
                      sem="stm", tag=(f"a2v_{pi}_{tbg}" if tb == 3 else None))
                evw = [f"a2v_{pi}_{tbg}", f"vones{vb}"]
                if vs >= 2:
                    pv2 = vs - 2
                    evw.append(f"a2vd_{pv2 // NTS}_{pv2 % NTS}")
                E("vector", lambda e, vb=vb: e.tensor_copy(
                    vstg[vb][:, :, :, 0:64],
                    pv[:].rearrange("p (a h k) -> p a h k", a=4, h=2)),
                  waits=evw, sem="svv", tag=f"a2vev_{pi}_{tbg}")
                for hh in range(2):
                    sc0 = st * NTI + tbg * 4
                    E("sync", lambda e, p=p, hh=hh, sc0=sc0, vb=vb: e.dma_start(
                        out=vhat_d[2 * p + hh, :, 65 * sc0: 65 * (sc0 + 4)].rearrange(
                            "p (a k) -> p a k", k=65),
                        in_=vstg[vb][:, :, hh, :]),
                      waits=[f"a2vev_{pi}_{tbg}"] if hh == 0 else [],
                      sem="svd", dma=True,
                      tag=(f"a2vd_{pi}_{tbg}" if hh == 1 else None))
            if pi + 3 < NST * NP:
                emit_wpl(pi + 3)



    def emit_pair_chains(st, p):
        pi = st * NP + p
        u1tag = f"u1ev_{(st + 1) * NTI - 1}"
        NR = 2 * NTS
        for part in range(2):
            sn = "q" if part == 0 else "k"
            spd = qsp_d if part == 0 else ksp_d
            E8 = E8q if part == 0 else E8k
            blkx = blkq if part == 0 else blkk
            sbsp = spl[2 * (pi % 2) + part]
            ps_seq = pi * 2 + part
            mb = 0
            for t in range(NTS):
                rmw = [f"a2sq_{pi}_{sn}_{t}"]
                if t == 0:
                    if ps_seq >= 1:
                        pp2, ppart2 = divmod(ps_seq - 1, 2)
                        rmw.append(f"a2ml_{pp2}_{'q' if ppart2 == 0 else 'k'}")
                    if pi == 0:
                        for t2 in range(NTS):
                            rmw += [f"bka{sn}_{t2}", f"bkb{sn}_{t2}", f"blkw{sn}_{t2}"]
                bsel = 2 * (pi % 2) + part
                E("tensor", lambda e, bsel=bsel, t=t, blkx=blkx: e.matmul(
                    prms[:], blkx[t][:], qsq[bsel][:, 512 * t:512 * (t + 1)],
                    start=(t == 0), stop=(t == NTS - 1)),
                  waits=rmw, sem="stm",
                  tag=(f"a2rmm_{pi}_{sn}" if t == NTS - 1 else None))
            rlw = [f"a2rmm_{pi}_{sn}"]
            if ps_seq >= 1:
                pp2, ppart2 = divmod(ps_seq - 1, 2)
                rlw.append(f"a2mv_{pp2}_{'q' if ppart2 == 0 else 'k'}")
            E("scalar", lambda e: e.activation(rrs[0:8, :], prms[0:8, :], AF.Ln,
                                               bias=cst[0:8, 1:2], scale=cst[0:8, 5:6]),
              waits=rlw, sem="ssc", tag=f"a2rl_{pi}_{sn}")
            E("scalar", lambda e: e.activation(rrs[0:8, :], rrs[0:8, :], AF.Exp, scale=cst[0:8, 2:3]),
              waits=[f"a2rl_{pi}_{sn}"], sem="ssc", tag=f"a2rr_{pi}_{sn}")
            mlw = [f"a2rr_{pi}_{sn}"]
            if ps_seq >= 1:
                pp2, ppart2 = divmod(ps_seq - 1, 2)
                mlw.append(f"a2mq_{pp2}" if ppart2 == 0 else f"a2mr_{pp2}")
            E("scalar", lambda e, mb=mb: e.activation(msr[mb][0:NR, :], prms[32:32 + NR, :],
                                                      AF.Ln, bias=cst[0:NR, 1:2]),
              waits=mlw, sem="ssc", tag=f"a2ml_{pi}_{sn}")
            E("scalar", lambda e, mb=mb: e.activation(msr[mb][0:NR, :], msr[mb][0:NR, :],
                                                      AF.Exp, scale=cst[0:NR, 3:4]),
              waits=[f"a2ml_{pi}_{sn}"], sem="ssc", tag=f"a2ms_{pi}_{sn}")
            E("vector", lambda e, mb=mb: e.tensor_mul(msr[mb][0:NR, :], msr[mb][0:NR, :],
                                                      rrs[0:NR, :]),
              waits=[f"a2ms_{pi}_{sn}"], sem="svv", tag=f"a2mv_{pi}_{sn}")
            if part == 0:
                E("sync", lambda e, mb=mb, p=p, st=st: e.dma_start(
                    out=mq_d[2 * p, ST * st:ST * (st + 1)].rearrange(
                        "(t c) -> t c", c=512),
                    in_=msr[mb][0:NTS, :]),
                  waits=[f"a2mv_{pi}_{sn}"], sem="smq", dma=True)
                E("sync", lambda e, mb=mb, p=p, st=st: e.dma_start(
                    out=mq_d[2 * p + 1, ST * st:ST * (st + 1)].rearrange(
                        "(t c) -> t c", c=512),
                    in_=msr[mb][NTS:NR, :]),
                  sem="smq", dma=True, tag=f"a2mq_{pi}")
            else:
                krw = [f"a2mv_{pi}_{sn}"]
                if pi >= 1:
                    krw.append(f"a2am_{pi - 1}")
                E("vector", lambda e, mb=mb: e.reduce_max(kmx8[0:NR, :], msr[mb][0:NR, :],
                                                          axis=AX),
                  waits=krw, sem="svv", tag=f"a2mr_{pi}")
                E("vector", lambda e: e.tensor_max(alpha8[0:NR, :], kmx8[0:NR, :],
                                                   alpha8[0:NR, :]),
                  waits=[f"a2mr_{pi}", "alz" if pi == 0 else f"a2am_{pi - 1}"],
                  sem="svv", tag=f"a2am_{pi}")
            for t in range(NTS):
                em = ps_seq * NTS + t
                emw = [f"a2rr_{pi}_{sn}", f"ld_e8{sn}_{t}"] + (["rrsz"] if pi == 0 else [])
                if em >= 2:
                    pe, te = divmod(em - 2, NTS)
                    pei, pep = divmod(pe, 2)
                    emw.append(f"a2n_{pei}_{'q' if pep == 0 else 'k'}_{te}")
                E("tensor", lambda e, E8=E8, t=t, em=em: e.matmul(
                    pexp[em % 2][:], E8[t][:], rrs[0:96, :], start=True, stop=True),
                  waits=emw, sem="stm", tag=f"a2emm_{pi}_{sn}_{t}")
                nw = [f"a2emm_{pi}_{sn}_{t}", f"a2raw_{pi}_{sn}_{t}"]
                if t == 0 and pi >= 2:
                    nw.append(f"a2sp_{pi - 2}_{sn}")
                bsel = 2 * (pi % 2) + part
                E("vector", lambda e, bsel=bsel, sbsp=sbsp, t=t, em=em: e.tensor_mul(
                    sbsp[:, 512 * t:512 * (t + 1)], qraw[bsel][:, 512 * t:512 * (t + 1)],
                    pexp[em % 2][:]),
                  waits=nw, sem="svv", tag=f"a2n_{pi}_{sn}_{t}")
            E("sync", lambda e, spd=spd, p=p, st=st, sbsp=sbsp: e.dma_start(
                out=spd[p, :, ST * st:ST * (st + 1)], in_=sbsp[:]),
              waits=[f"a2n_{pi}_{sn}_{NTS - 1}"], sem="ssp", dma=True,
              tag=f"a2sp_{pi}_{sn}")


    prev_ch = []
    NPAIR = NST * NP
    for pi0 in range(min(3, NPAIR)):
        emit_wpl(pi0)
    a1next = []
    for st in range(NST):
        if st == 0:
            P.merge(P.capture(emit_a1, 0))
        for p in range(NP):
            pi = st * NP + p
            if p == NP - 3 and st + 1 < NST:
                a1next = P.capture(emit_a1, st + 1)
            extra = []
            if a1next:
                n3 = (len(a1next) + NP - 1 - p) // (NP - p) if p >= NP - 3 else 0
                extra = a1next[:n3]
                a1next = a1next[n3:]
            mm = P.capture(emit_pair_mms, st, p)
            wchunk = wrest[:22]
            wrest = wrest[22:]
            P.merge(prev_ch, mm, extra, wchunk)
            prev_ch = P.capture(emit_pair_chains, st, p)
    P.merge(prev_ch, wrest)

    lastpi = NST * NP - 1
    E("sync", lambda e: e.nop(),
      waits=[f"a2sp_{lastpi}_k", f"a2vd_{lastpi}_{NTS - 1}", f"wst_{NW - 1}",
             f"wst_{NW - 2}", f"a2mq_{lastpi}"],
      sem="sad", tag="a2alldma")

    lastA = {
        "tensor": [f"a2v_{lastpi}_{NTS - 1}"],
        "vector": [f"a2vev_{lastpi}_{NTS - 1}", f"a2n_{lastpi}_k_{NTS - 1}",
                   f"a2am_{lastpi}"],
        "scalar": [f"a2ms_{lastpi}_k"],
        "gpsimd": [f"wld_{NW - 1}"],
        "sync": ["a2alldma"],
    }
    for eng in P.ENGINES:
        E(eng, lambda e: e.nop(), waits=lastA[eng], sem="sbar", tag=f"barA_{eng}")
    barA = [f"barA_{e}" for e in P.ENGINES]

    esA.close()

    esB = contextlib.ExitStack()
    woT = sbt("woT", [128, NCC, D], stk=esB)
    wrT = sbt("wrT", [128, NCC, D], stk=esB)
    wkT = sbt("wkT", [128, NCC, D], stk=esB)
    wvT = sbt("wvT", [128, NCC, D], stk=esB)
    ktil = [sbt(f"ktil{i}", [128, T], stk=esB) for i in range(3)]
    ksmp = [sbt(f"ksmp{i}", [128, 128], stk=esB) for i in range(3)]
    mqr = [sbt(f"mqr{i}", [4, 128], stk=esB) for i in range(3)]
    mrow = [sbt(f"mrow{i}", [4, 128], stk=esB) for i in range(2)]
    mneg4 = sbt("mneg4", [128, NQT], F32, stk=esB)
    negkr = sbt("negkr", [4, 128], F32, stk=esB)
    qtil = [sbt(f"qtil{i}", [128, TG], stk=esB) for i in range(3)]
    PT = [sbt(f"PT{i}", [128, 1024], stk=esB) for i in range(2)]
    vstr = [sbt(f"vstr{i}", [128, NBLK * 65], stk=esB) for i in range(3)]
    zstg = sbt("zstg", [65, 4, TG], stk=esB)
    lgat = [sbt(f"lgat{i}", [16, TG], stk=esB) for i in range(2)]
    lrec = sbt("lrec", [96, TG], stk=esB)
    ztg = [sbt(f"ztg{i}", [128, NCC, TG], stk=esB) for i in range(2)]
    xre = sbt("xre", [128, 2, D], stk=esB)
    x1 = sbt("x1", [128, NTB, D], F32, stk=esB)
    bn12b = sbt("bn12b", [128, 12], F32, stk=esB)
    bnagb = sbt("bnagb", [128, 2], F32, stk=esB)
    nrm2 = [sbt(f"nrm2{i}", [128, 2], F32, stk=esB) for i in range(2)]
    u2tm = [sbt(f"u2tm{i}", [128, D], F32, stk=esB) for i in range(2)]
    u2fm = sbt("u2fm", [128, NCC, TG], stk=esB)
    bcol = sbt("bcol", [128, NCC, 1], stk=esB)
    xr = sbt("xr", [128, NCC, TG], stk=esB)
    xk = sbt("xk", [128, NCC, TG], stk=esB)
    tmpv = sbt("tmpv", [128, NCC, 128], stk=esB)
    tmpg = sbt("tmpg", [128, NCC, 128], stk=esB)
    th = u2fm
    krelu = [sbt("krelu0", [128, 512], stk=esB)]
    kact = xr
    t1f = [sbt(f"t1f{i}", [128, 512], stk=esB) for i in range(2)]
    t2f = [sbt(f"t2f{i}", [128, 512], stk=esB) for i in range(2)]

    P2 = pst("P2", [128, 2048], stk=esB)
    zps = pst("zps", [65, TG], stk=esB)
    pms = pst("pms", [128, 512], stk=esB)
    tl = [pst(f"tl{i}", [128, 512], stk=esB) for i in range(2)]

    tlst = {0: None, 1: None}
    tlrr = [0]

    def tl_use(extra):
        par = tlrr[0]
        tlrr[0] = 1 - par
        w = list(extra)
        if tlst[par] is not None:
            w.append(tlst[par])
        return par, w

    def tl_done(par, tag):
        tlst[par] = tag

    for i in range(3):
        E("gpsimd", lambda e, i=i: e.memset(ktil[i][64:128, :], 0.0), waits=barA,
          sem="sgp", tag=f"kz_{i}")
        E("gpsimd", lambda e, i=i: e.memset(ktil[i][64:65, :], 1.0), waits=[f"kz_{i}"],
          sem="sgp", tag=f"k1_{i}")
        E("gpsimd", lambda e, i=i: e.memset(qtil[i][64:128, :], 0.0), sem="sgp",
          tag=f"qz_{i}")
        E("gpsimd", lambda e, i=i: e.memset(ksmp[i][64:128, :], 0.0), sem="sgp",
          tag=f"ksz_{i}")
    E("vector", lambda e: e.memset(bcol[:], 0.0), waits=barA, sem="svv",
      tag="bnd_init")
    E("vector", lambda e: e.memset(lrec[0:96, :], 0.0), waits=barA, sem="svv",
      tag="lrec_init")
    bwl_srcs = [(woT, wTo_d), (wrT, wTr_d), (wkT, wTk_d), (wvT, wTv_d)]
    bwl_ix = [0]

    def emit_bwl(nchunks):
        for _ in range(nchunks):
            ci = bwl_ix[0]
            if ci >= 4 * NCC:
                return
            bwl_ix[0] = ci + 1
            wn, cc = divmod(ci, NCC)
            wsb, wd = bwl_srcs[wn]
            E("sync", lambda e, wsb=wsb, wd=wd, cc=cc: e.dma_start(
                out=wsb[:, cc, :], in_=wd[cc, :, :]),
              waits=barA, sem="swl2", dma=True,
              tag=(f"bwl_{wn}" if cc == NCC - 1 else None))

    E("tensor", lambda e: e.transpose(tl[0][0:1, 0:8], alpha8[0:8, :],
                                      identf[0:8, 0:8]),
      waits=barA, sem="stm", tag="batr")
    E("vector", lambda e: e.reduce_max(arow[:], tl[0][0:1, 0:8], axis=AX),
      waits=["batr"], sem="svv", tag="bamx")
    E("scalar", lambda e: e.activation(alpham[:], arow[:], AF.Identity,
                                       scale=cst[0:1, 6:7]),
      waits=["bamx"], sem="ssc", tag="basc")
    E("vector", lambda e: e.tensor_copy(alphab[:], alpham[:]), waits=["basc"],
      sem="svv", tag="babf")
    E("tensor", lambda e: e.matmul(tl[0][:, 0:1], ones128[0:1, :], alphab[:],
                                   start=True, stop=True),
      waits=["babf"], sem="stm", tag="babc")
    E("scalar", lambda e: e.copy(a16[:], tl[0][:, 0:1]), waits=["babc"], sem="ssc",
      tag="ba16")
    tlst[0] = "ba16"

    def kql(u, waits):
        ub = u % 3
        tg, h = divmod(u, H)
        E("sync", lambda e, h=h, ub=ub: e.dma_start(
            out=ktil[ub][0:64, :], in_=ksp_d[h // 2, 64 * (h % 2):64 * (h % 2) + 64, :]),
          waits=waits, sem=f"skl{ub}", dma=True, tag=f"bkl_{u}")
        E("sync", lambda e, h=h, tg=tg, ub=ub: e.dma_start(
            out=qtil[ub][0:64, :],
            in_=qsp_d[h // 2, 64 * (h % 2):64 * (h % 2) + 64, TG * tg:TG * (tg + 1)]),
          sem=f"skl{ub}", dma=True, tag=f"bql_{u}")
        c0 = (T // NTG) * ((tg + 3) % NTG) + (128 if T > 256 else 0)
        E("sync", lambda e, h=h, tg=tg, ub=ub: e.dma_start(
            out=mqr[ub][0:NQT, :],
            in_=mq_d[h, TG * tg:TG * (tg + 1)].rearrange("(q c) -> q c", c=128)),
          sem=f"skl{ub}", dma=True, tag=f"bmc_{u}")
        E("sync", lambda e, h=h, ub=ub, c0=c0: e.dma_start(
            out=ksmp[ub][0:64, :],
            in_=ksp_d[h // 2, 64 * (h % 2):64 * (h % 2) + 64, c0:c0 + 128]),
          waits=(waits if u < 3 else []), sem=f"skl{ub}", dma=True, tag=f"bkls_{u}")
        E("sync", lambda e, h=h, ub=ub: e.dma_start(out=vstr[ub][:], in_=vhat_d[h, :, :]),
          sem=f"skl{ub}", dma=True, tag=f"bvl_{u}")

    kql(0, barA + ["qz_0"])
    if NU > 1:
        kql(1, ["qz_1"])
    if NU > 2:
        kql(2, ["qz_2"])

    def pv_emit(u, sc, ub):
        w = [f"bexp_{u}_{sc}"]
        if sc == 0:
            w.append(f"bvl_{u}")
            if u >= 1:
                w.append(f"bzev_{u - 1}")
        E("tensor", lambda e, u=u, ub=ub, sc=sc: e.matmul(
            zps[:], vstr[ub][:, 65 * sc:65 * (sc + 1)],
            PT[(sc // 2) % 2][:, 512 * (sc % 2):512 * (sc % 2 + 1)],
            start=(sc == 0), stop=(sc == NSC - 1)),
          waits=w, sem="stm", tag=f"bpv_{u}_{sc}")

    def mchain(u):
        tg, h = divmod(u, H)
        ub = u % 3
        w0 = [f"bql_{u}", f"bkls_{u}"]
        if u < 3:
            w0 += [f"qz_{ub}", f"ksz_{ub}"]
        if u >= 1:
            w0.append(f"bmcp_{u - 1}")
        for qt in range(NQT):
            E("tensor", lambda e, ub=ub, qt=qt: e.matmul(
                pms[:, 128 * qt:128 * (qt + 1)],
                qtil[ub][0:128, 128 * qt:128 * (qt + 1)], ksmp[ub][0:128, :],
                start=True, stop=True),
              waits=w0 if qt == 0 else [], sem="stm", tag=f"bsm_{u}_{qt}")
        for qt in range(NQT):
            rw = [f"bsm_{u}_{NQT - 1}"]
            if qt == 0 and u >= 1:
                rw.append(f"bmtr_{u - 1}")
            E("vector", lambda e, qt=qt: e.reduce_max(
                mneg4[:, qt:qt + 1], pms[:, 128 * qt:128 * (qt + 1)],
                axis=AX, negate=True),
              waits=rw, sem="svv", tag=f"bsr_{u}_{qt}")
        kw = [f"bmc_{u}", "ba16"]
        if u >= 1:
            kw.append(f"bmcp_{u - 1}")
        E("vector", lambda e, ub=ub: e.tensor_scalar(
            negkr[:], mqr[ub][0:NQT, :], a16[0:4, 0:1], 70.0, ALU.mult, ALU.add),
          waits=kw, sem="svv", tag=f"bnk_{u}")
        E("tensor", lambda e: e.transpose(pms[0:NQT, 0:128],
                                          mneg4[:, 0:NQT], identf[:]),
          waits=[f"bsr_{u}_{NQT - 1}"], sem="stm", tag=f"bmtr_{u}")
        cw = [f"bmtr_{u}", f"bnk_{u}"]
        if u >= 2:
            cw.append(f"bmd_{u - 2}")
        E("vector", lambda e, u=u: e.tensor_tensor(
            mrow[u % 2][:], pms[0:NQT, 0:128], negkr[:], ALU.min),
          waits=cw, sem="svv", tag=f"bmcp_{u}")
        E("scalar", lambda e, u=u, ub=ub: e.dma_start(
            out=qtil[ub][64:65, :], in_=mrow[u % 2][0:NQT, :]),
          waits=[f"bmcp_{u}"] + ([f"qz_{ub}"] if u < 3 else []),
          sem="smd", dma=True, tag=f"bmd_{u}")

    def zev_block(u):
        tg, h = divmod(u, H)
        slot = h % 4
        zw = [f"bpv_{u}_{NSC - 1}"]
        if u >= 4:
            zw.append(f"bzrp_{u - 4}")
            zw.append(f"blg_{(u - 4) // 4}")
        E("vector", lambda e, slot=slot: e.tensor_copy(zstg[:, slot, :], zps[:]),
          waits=zw, sem="svv", tag=f"bzev_{u}")
        rw = [f"bzev_{u}"]
        if tg >= 2:
            rw.append(f"bwom_{tg - 2}_7")
        E("sync", lambda e, h=h, tg=tg, slot=slot: e.dma_start(
            out=ztg[tg % 2][64 * (h % 2):64 * (h % 2) + 64, h // 2, :],
            in_=zstg[0:64, slot, :]),
          waits=rw, sem="szr", dma=True, tag=f"bzrp_{u}")
        if h % 4 == 3:
            hf = h // 4
            lw = [f"bzev_{u}"]
            if tg >= 2:
                lw.append(f"blrc_{tg - 2}")
            E("sync", lambda e, hf=hf, tg=tg: e.dma_start(
                out=lgat[tg % 2][4 * hf:4 * (hf + 1), :],
                in_=zstg[64:65, :, :]),
              waits=lw, sem="slg", dma=True, tag=f"blg_{u // 4}")

    def unit_carry(up):
        ubp = up % 3
        pv_emit(up, NSC - 2, ubp)
        pv_emit(up, NSC - 1, ubp)
        zev_block(up)
        if up + 3 < NU:
            kql(up + 3, [f"bpv_{up}_{NSC - 1}"])
        if up + 2 < NU:
            mchain(up + 2)
        if up >= 1:
            emit_bwl(3)

    def unit_p2(u):
        tg, h = divmod(u, H)
        ub = u % 3
        for sc in range(NSC):
            bank = sc % 4
            w = []
            if sc == 0:
                w = [f"bmd_{u}"]
            j4 = sc - 4
            if j4 >= 0:
                w.append(f"bexp_{u}_{j4}")
            elif u >= 1:
                w.append(f"bexp_{u - 1}_{NSC - 4 + sc}")
            E("tensor", lambda e, ub=ub, sc=sc, bank=bank: e.matmul(
                P2[:, 512 * bank:512 * (bank + 1)],
                ktil[ub][0:128, 128 * sc:128 * (sc + 1)], qtil[ub][0:128, :],
                start=True, stop=True),
              waits=w, sem="stm", tag=f"bp2m_{u}_{sc}")
            if sc == 2 and u >= 1:
                unit_carry(u - 1)
            ew = [f"bp2m_{u}_{sc}"]
            if sc >= 4:
                ew.append(f"bpv_{u}_{sc - 4}")
            elif u >= 1:
                ew.append(f"bpv_{u - 1}_{NSC - 4 + sc}")
            E("scalar", lambda e, ub=ub, sc=sc, bank=bank: e.activation(
                PT[(sc // 2) % 2][:, 512 * (sc % 2):512 * (sc % 2 + 1)],
                P2[:, 512 * bank:512 * (bank + 1)], AF.Exp),
              waits=ew, sem="sse", tag=f"bexp_{u}_{sc}")
            if sc >= 2:
                pv_emit(u, sc - 2, ub)

    def tail(tg):
        if debug_taps:
            E("sync", lambda e, tg=tg: e.dma_start(out=ltap_d[tg], in_=lgat[tg % 2][:]),
              waits=[f"blg_{tg * 4 + q}" for q in range(4)], sem="slg", dma=True)
            E("sync", lambda e, tg=tg: e.dma_start(out=ztap_d[tg], in_=zstg[:]),
              waits=[f"blg_{tg * 4 + q}" for q in range(4)], sem="slg", dma=True)
        def _lrec(e):
            with nc.allow_low_precision(reason="1/l stored bf16 (same as prior path)"):
                return e.reciprocal(lrec[0:16, :], lgat[tg % 2][:])
        E("vector", _lrec,
          waits=[f"blg_{tg * 4 + q}" for q in range(4)] +
                ([OPT(f"blrm_{tg - 1}_{NCC - 1}")] if tg >= 1 else []),
          sem="svv", tag=f"blrc_{tg}")
        for j in range(NCC):
            par, w = tl_use([f"blrc_{tg}", f"bzrp_{tg * H + 2 * j}",
                             f"bzrp_{tg * H + 2 * j + 1}"] +
                            (["lrec_init"] if tg == 0 and j == 0 else []))
            E("tensor", lambda e, j=j, par=par: e.matmul(
                tl[par][:], E2[0:96, 128 * j:128 * (j + 1)], lrec[0:96, 0:TG],
                start=True, stop=True),
              waits=w, sem="stm", tag=f"blrm_{tg}_{j}")
            E("vector", lambda e, j=j, par=par, tg=tg: e.tensor_mul(
                ztg[tg % 2][:, j, :], ztg[tg % 2][:, j, :], tl[par][:]),
              waits=[f"blrm_{tg}_{j}"], sem="svv", tag=f"bzn_{tg}_{j}")
            tl_done(par, f"bzn_{tg}_{j}")
        def xre_load(tb2):
            gtb2 = tg * NTB + tb2
            w = []
            if tb2 >= 2:
                w += [f"bx1_{tg}_{2 * (tb2 - 2)}", f"bx1_{tg}_{2 * (tb2 - 2) + 1}"]
            elif tg >= 1:
                w += [OPT(f"bx1_{tg - 1}_{2 * (tb2 + 2)}"), OPT(f"bx1_{tg - 1}_{2 * (tb2 + 2) + 1}")]
            E("gpsimd", lambda e, tg=tg, tb2=tb2: e.dma_start(
                out=xre[:, tb2 % 2, :], in_=x_t[:, tg * NTB + tb2, :]),
              waits=w, sem=f"sxr{tb2 % 2}", dma=True, tag=f"bxl_{gtb2}")

        for tb in range(min(2, NTB)):
            xre_load(tb)

        for tb in range(NTB):
            gtb = tg * NTB + tb
            for dg in range(2):
                i = tb * 2 + dg
                par, w = tl_use([f"bzn_{tg}_{NCC - 1}", "bwl_0"])
                for cc in range(NCC):
                    E("tensor", lambda e, cc=cc, tb=tb, dg=dg, par=par, tg=tg: e.matmul(
                        tl[par][:], ztg[tg % 2][:, cc, 128 * tb:128 * (tb + 1)],
                        woT[:, cc, 512 * dg:512 * (dg + 1)],
                        start=(cc == 0), stop=(cc == NCC - 1)),
                      waits=w if cc == 0 else [], sem="stm",
                      tag=(f"bwom_{tg}_{i}" if cc == NCC - 1 else None))
                wev = [f"bwom_{tg}_{i}", "borev_1"]
                if i >= 2:
                    wev.append(OPT(f"bx1_{tg}_{i - 2}"))
                elif tg >= 1:
                    wev.append(OPT(f"bx1_{tg - 1}_{i + 6}"))
                E("vector", lambda e, par=par, dg=dg, i=i: e.tensor_add(
                    t1f[i % 2][:], tl[par][:], borep[:, 512 * dg:512 * (dg + 1)]),
                  waits=wev, sem="svv", tag=f"bwoe_{tg}_{i}")
                tl_done(par, f"bwoe_{tg}_{i}")
                xw = [f"bwoe_{tg}_{i}", f"bxl_{gtb}"]
                if tg >= 1:
                    xw.append(OPT(f"bod_{gtb - NTB}"))
                E("gpsimd", lambda e, tb=tb, dg=dg, i=i: e.tensor_add(
                    x1[:, tb, 512 * dg:512 * (dg + 1)], t1f[i % 2][:],
                    xre[:, tb % 2, 512 * dg:512 * (dg + 1)]),
                  waits=xw, sem="sgp", tag=f"bx1_{tg}_{i}")
            if debug_taps:
                E("sync", lambda e, tg=tg, tb=tb: e.dma_start(
                    out=x1tap_d.rearrange("(ti p) c -> p ti c", p=128)[:, tg * NTB + tb, :],
                    in_=x1[:, tb, :]),
                  waits=[f"bx1_{tg}_{2 * tb}", f"bx1_{tg}_{2 * tb + 1}"],
                  sem="sxt", dma=True, tag=f"bx1t_{gtb}")
            if tb + 2 < NTB:
                xre_load(tb + 2)
            nb = gtb % 2
            bw = [f"bx1_{tg}_{2 * tb}", f"bx1_{tg}_{2 * tb + 1}"]
            if gtb >= 1:
                bw.append(f"bnrm2_{gtb - 1}")
            E("vector", lambda e, tb=tb: e.bn_stats(bn12b[:, 0:6], x1[:, tb, 0:512]),
              waits=bw, sem="svv", tag=f"bbs1_{gtb}")
            E("vector", lambda e, tb=tb: e.bn_stats(bn12b[:, 6:12], x1[:, tb, 512:1024]),
              sem="svv", tag=f"bbs2_{gtb}")
            E("vector", lambda e: e.bn_aggr(bnagb[:], bn12b[:]),
              waits=[f"bbs1_{gtb}", f"bbs2_{gtb}"], sem="svv", tag=f"bbn2_{gtb}")
            sw = [f"bbn2_{gtb}"]
            if gtb >= 2:
                sw.append(OPT(f"bu2n_{gtb - 2}"))
            E("scalar", lambda e, nb=nb: e.activation(nrm2[nb][:, 1:2], bnagb[:, 1:2],
                                                      AF.Ln, bias=cst[:, 0:1]),
              waits=sw, sem="ssc", tag=f"bnl_{gtb}")
            E("scalar", lambda e, nb=nb: e.activation(nrm2[nb][:, 1:2], nrm2[nb][:, 1:2],
                                                      AF.Exp, scale=cst[:, 2:3]),
              waits=[f"bnl_{gtb}"], sem="ssc", tag=f"bne_{gtb}")
            E("scalar", lambda e, nb=nb: e.copy(nrm2[nb][:, 0:1], bnagb[:, 0:1]),
              waits=[f"bne_{gtb}"], sem="ssc", tag=f"bnrm2_{gtb}")
            uw = [f"bnrm2_{gtb}"]
            if gtb >= 2:
                uw.append(OPT(f"bu2tr_{gtb - 2}_1"))
            E("vector", lambda e, tb=tb, nb=nb, gtb=gtb: e.tensor_scalar(
                u2tm[gtb % 2][:], x1[:, tb, :], nrm2[nb][:, 0:1], nrm2[nb][:, 1:2],
                ALU.subtract, ALU.mult),
              waits=uw, sem="svv", tag=f"bu2n_{gtb}")
            for gg2 in range(2):
                par, w = tl_use([f"bu2n_{gtb}"])
                for blk in range(4):
                    E("tensor", lambda e, gtb=gtb, gg2=gg2, blk=blk, par=par: e.transpose(
                        tl[par][:, 128 * blk:128 * (blk + 1)],
                        u2tm[gtb % 2][:, 512 * gg2 + 128 * blk: 512 * gg2 + 128 * (blk + 1)],
                        identf[:]),
                      waits=w if blk == 0 else [], sem="stm",
                      tag=(f"bu2tr_{gtb}_{gg2}" if blk == 3 else None))
                ew = [f"bu2tr_{gtb}_{gg2}"]
                if tg >= 1:
                    ew.append(OPT(f"bt2_{tg - 1}_7"))
                else:
                    ew.append("bnd_init")
                E("vector", lambda e, tb=tb, gg2=gg2, par=par: e.tensor_copy(
                    u2fm[:, 4 * gg2:4 * (gg2 + 1), 128 * tb:128 * (tb + 1)],
                    tl[par][:].rearrange("p (c k) -> p c k", k=128)),
                  waits=ew, sem="svv", tag=f"bu2e_{gtb}_{gg2}")
                tl_done(par, f"bu2e_{gtb}_{gg2}")
            xrw = [f"bu2e_{gtb}_0", f"bu2e_{gtb}_1", "mfm_7_15"]
            if tg >= 1:
                xrw.append(OPT(f"bwvm_{tg - 1}_7"))
            if tb == 0:
                xrw.append("bnd_init" if tg == 0 else f"bbndc_{tg - 1}")
            lo = 128 * tb
            if tb == 0:
                E("vector", lambda e: e.tensor_sub(tmpv[:, :, 1:128],
                                                   u2fm[:, :, 1:128], u2fm[:, :, 0:127]),
                  waits=xrw, sem="svv")
                E("vector", lambda e: e.tensor_sub(tmpv[:, :, 0:1], u2fm[:, :, 0:1], bcol[:]),
                  sem="svv", tag=f"bxrm1_{gtb}")
            else:
                E("vector", lambda e, lo=lo: e.tensor_sub(
                    tmpv[:], u2fm[:, :, lo:lo + 128], u2fm[:, :, lo - 1:lo + 127]),
                  waits=xrw, sem="svv", tag=f"bxrm1_{gtb}")
            E("vector", lambda e: e.tensor_mul(tmpv[:], tmpv[:], mufm[:]),
              waits=[f"bxrm1_{gtb}"], sem="svv", tag=f"bxrm2_{gtb}")
            if tb == 0:
                E("vector", lambda e, lo=lo: e.tensor_add(
                    xr[:, :, lo + 1:lo + 128], tmpv[:, :, 1:128], u2fm[:, :, 0:127]),
                  waits=[f"bxrm2_{gtb}"], sem="svv")
                E("vector", lambda e, lo=lo: e.tensor_add(
                    xr[:, :, lo:lo + 1], tmpv[:, :, 0:1], bcol[:]),
                  sem="svv", tag=f"bxr_{gtb}")
            else:
                E("vector", lambda e, lo=lo: e.tensor_add(
                    xr[:, :, lo:lo + 128], tmpv[:], u2fm[:, :, lo - 1:lo + 127]),
                  waits=[f"bxrm2_{gtb}"], sem="svv", tag=f"bxr_{gtb}")
            xkw = [f"bu2e_{gtb}_0", f"bu2e_{gtb}_1", "mfm_7_15"]
            if tg >= 1:
                xkw.append(OPT(f"bwkm_{tg - 1}_7"))
            if tb == 0:
                xkw.append("bnd_init" if tg == 0 else f"bbndc_{tg - 1}")
            if tb == 0:
                E("gpsimd", lambda e: e.tensor_sub(tmpg[:, :, 1:128],
                                                   u2fm[:, :, 1:128], u2fm[:, :, 0:127]),
                  waits=xkw, sem="sgp")
                E("gpsimd", lambda e: e.tensor_sub(tmpg[:, :, 0:1], u2fm[:, :, 0:1], bcol[:]),
                  sem="sgp", tag=f"bxkm1_{gtb}")
            else:
                E("gpsimd", lambda e, lo=lo: e.tensor_sub(
                    tmpg[:], u2fm[:, :, lo:lo + 128], u2fm[:, :, lo - 1:lo + 127]),
                  waits=xkw, sem="sgp", tag=f"bxkm1_{gtb}")
            E("gpsimd", lambda e: e.tensor_mul(tmpg[:], tmpg[:], mkfm[:]),
              waits=[f"bxkm1_{gtb}"], sem="sgp", tag=f"bxkm2_{gtb}")
            if tb == 0:
                E("gpsimd", lambda e, lo=lo: e.tensor_add(
                    xk[:, :, lo + 1:lo + 128], tmpg[:, :, 1:128], u2fm[:, :, 0:127]),
                  waits=[f"bxkm2_{gtb}"], sem="sgp")
                E("gpsimd", lambda e, lo=lo: e.tensor_add(
                    xk[:, :, lo:lo + 1], tmpg[:, :, 0:1], bcol[:]),
                  sem="sgp", tag=f"bxk_{gtb}")
            else:
                E("gpsimd", lambda e, lo=lo: e.tensor_add(
                    xk[:, :, lo:lo + 128], tmpg[:], u2fm[:, :, lo - 1:lo + 127]),
                  waits=[f"bxkm2_{gtb}"], sem="sgp", tag=f"bxk_{gtb}")
        E("vector", lambda e: e.tensor_copy(bcol[:], u2fm[:, :, TG - 1:TG]),
          waits=[f"bxr_{tg * NTB + NTB - 1}", f"bxk_{tg * NTB + NTB - 1}"],
          sem="svv", tag=f"bbndc_{tg}")

        for tb in range(NTB):
            gtb = tg * NTB + tb
            for dg in range(2):
                i = tb * 2 + dg
                par, w = tl_use([f"bxr_{gtb}", "bwl_1"])
                if tg >= 1:
                    w.append(OPT(f"bt2_{tg - 1}_{i}"))
                if i == 0:
                    w.append(f"bbndc_{tg}")
                for cc in range(NCC):
                    E("tensor", lambda e, cc=cc, tb=tb, dg=dg, par=par: e.matmul(
                        tl[par][:], xr[:, cc, 128 * tb:128 * (tb + 1)],
                        wrT[:, cc, 512 * dg:512 * (dg + 1)],
                        start=(cc == 0), stop=(cc == NCC - 1)),
                      waits=w if cc == 0 else [], sem="stm",
                      tag=(f"bwrm_{tg}_{i}" if cc == NCC - 1 else None))
                E("scalar", lambda e, i=i, par=par: e.activation(
                    th[:, i, :], tl[par][:], AF.Tanh, scale=cst[:, 3:4]),
                  waits=[f"bwrm_{tg}_{i}"], sem="ssc", tag=f"bth_{tg}_{i}")
                tl_done(par, f"bth_{tg}_{i}")
        for dc in range(NCC):
            par, w = tl_use([f"bxk_{tg * NTB + NTB - 1}", "bwl_2"])
            if dc == 0:
                w.append(f"bwrm_{tg}_7")
            for cc in range(NCC):
                E("tensor", lambda e, cc=cc, dc=dc, par=par: e.matmul(
                    tl[par][:], wkT[:, cc, 128 * dc:128 * (dc + 1)], xk[:, cc, :],
                    start=(cc == 0), stop=(cc == NCC - 1)),
                  waits=w if cc == 0 else [], sem="stm",
                  tag=(f"bwkm_{tg}_{dc}" if cc == NCC - 1 else None))
            kw = [f"bwkm_{tg}_{dc}"]
            if dc >= 1:
                kw.append(f"bksq_{tg}_{dc - 1}")
            E("vector", lambda e, dc=dc, par=par: e.tensor_scalar_max(
                krelu[0][:], tl[par][:], 0.0),
              waits=kw, sem="svv", tag=f"bkrl_{tg}_{dc}")
            tl_done(par, f"bkrl_{tg}_{dc}")
            kqw = [f"bkrl_{tg}_{dc}"]
            if dc == 0:
                kqw.append(f"bwrm_{tg}_7")
            E("vector", lambda e, dc=dc: e.tensor_mul(
                kact[:, dc, :], krelu[0][:], krelu[0][:]),
              waits=kqw, sem="svv", tag=f"bksq_{tg}_{dc}")
        for tb in range(NTB):
            gtb = tg * NTB + tb
            for dg in range(2):
                i = tb * 2 + dg
                par, w = tl_use([f"bksq_{tg}_{NCC - 1}", "bwl_3"])
                for dk in range(NCC):
                    E("tensor", lambda e, dk=dk, tb=tb, dg=dg, par=par: e.matmul(
                        tl[par][:], kact[:, dk, 128 * tb:128 * (tb + 1)],
                        wvT[:, dk, 512 * dg:512 * (dg + 1)],
                        start=(dk == 0), stop=(dk == NCC - 1)),
                      waits=w if dk == 0 else [], sem="stm",
                      tag=(f"bwvm_{tg}_{i}" if dk == NCC - 1 else None))
                vw = [f"bwvm_{tg}_{i}", f"bth_{tg}_{i}"]
                if i >= 2:
                    vw.append(f"bfin_{tg * NTB + (i - 2) // 2}_{(i - 2) % 2}")
                elif tg >= 1:
                    vw.append(OPT(f"bfin_{(tg - 1) * NTB + (i + 6) // 2}_{(i + 6) % 2}"))
                E("vector", lambda e, par=par, i=i: e.tensor_scalar(
                    t2f[i % 2][:], th[:, i, :], cst[:, 3:4], cst[:, 3:4],
                    ALU.mult, ALU.add),
                  waits=vw, sem="svv", tag=f"bsg_{tg}_{i}")
                E("vector", lambda e, par=par, i=i: e.tensor_mul(
                    t2f[i % 2][:], t2f[i % 2][:], tl[par][:]),
                  waits=[f"bsg_{tg}_{i}"], sem="svv", tag=f"bt2_{tg}_{i}")
                tl_done(par, f"bt2_{tg}_{i}")
                E("gpsimd", lambda e, tb=tb, dg=dg, i=i: e.tensor_add(
                    x1[:, tb, 512 * dg:512 * (dg + 1)], t2f[i % 2][:],
                    x1[:, tb, 512 * dg:512 * (dg + 1)]),
                  waits=[f"bt2_{tg}_{i}", f"bx1_{tg}_{i}"] +
                        ([OPT(f"bx1t_{gtb}")] if debug_taps else []),
                  sem="sgp", tag=f"bfin_{gtb}_{dg}")
            E("sync", lambda e, tg=tg, tb=tb: e.dma_start(
                out=out_t[:, tg * NTB + tb, :], in_=x1[:, tb, :]),
              waits=[f"bfin_{gtb}_0", f"bfin_{gtb}_1"], sem="sod", dma=True,
              tag=f"bod_{gtb}")

    mchain(0)
    if NU > 1:
        mchain(1)
    pend_tail = []
    tq = None
    for u in range(NU):
        l2 = P.capture(unit_p2, u)
        if u == NU - 1:
            l2 += P.capture(unit_carry, u)
        take = len(pend_tail) if len(pend_tail) <= 16 else max(8, len(pend_tail) // 16)
        half = pend_tail[:take]
        rest = pend_tail[take:]
        P.merge(l2, half)
        pend_tail = rest
        if tq is not None:
            pend_tail = pend_tail + tq
            tq = None
        tg, h = divmod(u, H)
        if h == H - 1:
            tq = P.capture(tail, tg)
    emit_bwl(4 * NCC)
    P.merge(pend_tail + (tq or []))

    E("sync", lambda e: e.nop(), waits=[("sod", 16 * NTG * NTB)], sem="sfin", tag="fin")

    with nc.Block() as block:
        P.run(block)
    esB.close()
    es.close()
    return nc


def kernel(L0_attn_x, L0_attn_Wqkv, L0_attn_bqkv, L0_attn_Wo, L0_attn_bo,
           L0_attn_wq, L0_attn_wk, L0_ffn_W_r, L0_ffn_W_k, L0_ffn_W_v,
           L0_ffn_mu_r, L0_ffn_mu_k):
    x = np.ascontiguousarray(np.asarray(L0_attn_x, dtype=np.float32))
    B, T, _ = x.shape
    nc = build_nc(T=T)
    shared = {
        "wqkv": np.ascontiguousarray(L0_attn_Wqkv, dtype=np.float32),
        "bqkv": np.ascontiguousarray(L0_attn_bqkv, dtype=np.float32),
        "wo": np.ascontiguousarray(L0_attn_Wo, dtype=np.float32),
        "bo": np.ascontiguousarray(L0_attn_bo, dtype=np.float32),
        "wqg": np.ascontiguousarray(L0_attn_wq, dtype=np.float32),
        "wkg": np.ascontiguousarray(L0_attn_wk, dtype=np.float32),
        "wr": np.ascontiguousarray(L0_ffn_W_r, dtype=np.float32),
        "wkf": np.ascontiguousarray(L0_ffn_W_k, dtype=np.float32),
        "wv": np.ascontiguousarray(L0_ffn_W_v, dtype=np.float32),
        "mur": np.ascontiguousarray(L0_ffn_mu_r, dtype=np.float32),
        "muk": np.ascontiguousarray(L0_ffn_mu_k, dtype=np.float32),
    }
    in_maps = [dict(shared, x=np.ascontiguousarray(x[b])) for b in range(B)]
    res = run_bass_kernel_spmd(nc, in_maps, core_ids=list(range(B)))
    out = np.stack([np.asarray(res.results[b]["out"], dtype=np.float32)
                    for b in range(B)], axis=0)
    return out
